# version 19
# baseline (speedup 1.0000x reference)
import math
import numpy as np
import ml_dtypes
from contextlib import ExitStack

import concourse.bass as bass
import concourse.mybir as mybir
from concourse.bass_utils import run_bass_kernel_spmd

F32 = mybir.dt.float32
U16 = mybir.dt.uint16
BF16 = mybir.dt.bfloat16
AF = mybir.ActivationFunctionType
ALU = mybir.AluOpType
AX = mybir.AxisListType
BF = ml_dtypes.bfloat16

NCORES = 8
B_PER = 2
L = 2048
E = 2048
HQ, HKV, DH = 16, 2, 128
HID = 8192
SCALE = 1.0 / math.sqrt(DH)
LN_EPS = 1e-5
RMS_EPS = float(np.finfo(np.float32).eps)
NEG = -1e30

ENG_SEM = {"tensor": "PE", "vector": "DV", "scalar": "AC"}
DMA_ENG = ("sync", "gpsimd")


class Prog:
    def __init__(self, nc):
        self.nc = nc
        self.sched = {e: [] for e in ("sync", "gpsimd", "tensor", "vector", "scalar")}
        self.counts = {s: 0 for s in ("PE", "DV", "AC")}
        self.waited = {e: {} for e in self.sched}
        self.sems = {}
        self.alloc_sem = None
        self.ninstr = 0

    def _sem(self, name):
        if name not in self.sems:
            self.sems[name] = self.alloc_sem(name)
            self.counts.setdefault(name, 0)
        return self.sems[name]

    def wait(self, eng, sem, val):
        if val <= 0:
            return
        if val <= self.waited[eng].get(sem, 0):
            return
        assert val <= self.counts[sem], (eng, sem, val, self.counts[sem])
        self.waited[eng][sem] = val
        self.sched[eng].append(lambda e, s=sem, v=val: e.wait_ge(self.sems[s], v))
        self.ninstr += 1

    def run(self, eng, fn, inc=True, dsem=None):
        if eng in DMA_ENG and dsem is not None:
            sem = dsem
            self._sem(sem)
            n = 16
        elif eng == "gpsimd":
            sem = "PL"
            self._sem(sem)
            n = 1
        else:
            sem = ENG_SEM[eng]
            n = 1
        tok = None
        if inc:
            self.counts[sem] += n
            tok = (sem, self.counts[sem])

        def wrapped(e, fn=fn, tok=tok, sem=sem, n=n):
            ins = fn(e)
            if tok is not None:
                ins.then_inc(self.sems[sem], n)
        self.sched[eng].append(wrapped)
        self.ninstr += 1
        return tok

    def barrier(self):
        snap = dict(self.counts)
        for e in self.sched:
            for s, v in snap.items():
                self.wait(e, s, v)


class Rg:
    __slots__ = ("w", "r")

    def __init__(self):
        self.w = None
        self.r = []

    def read(self, prog, eng):
        if self.w:
            prog.wait(eng, *self.w)

    def noted_read(self, tok):
        if tok:
            self.r.append(tok)

    def write(self, prog, eng):
        best = {}
        if self.w:
            best[self.w[0]] = self.w[1]
        for s_, v_ in self.r:
            if v_ > best.get(s_, 0):
                best[s_] = v_
        for s_, v_ in best.items():
            prog.wait(eng, s_, v_)

    def noted_write(self, tok):
        self.w = tok
        self.r = []


def build_program(debug=False):
    nc = bass.Bass("TRN2", target_bir_lowering=False)
    p = Prog(nc)

    def din(name, shape, dt=F32):
        return nc.dram_tensor(name, shape, dt, kind="ExternalInput")

    x_h = din("x", [B_PER * L, E])
    wq_h = din("wq_fb", [HQ, 128, E])
    wk_h = din("wk_fb", [HKV, 128, E])
    wv_h = din("wv_rb", [128, 16 * 256])
    wo_h = din("wo_qb", [4, 128, 16 * 512])
    bq_h = din("bq_t", [128, 16])
    bk_h = din("bk_t", [128, 2])
    bv_h = din("bv_row", [1, 256])
    wfcT_h = din("wfcT", [E, HID])
    wprT_h = din("wprT", [HID, E])
    ident_h = din("ident", [128, 128], BF16)
    onesrow_h = din("onesrow", [1, 128], BF16)
    onescol_h = din("onescol", [128, 1], BF16)
    gmask_h = din("gmask", [128, 128], BF16)
    indic2_h = din("indic2", [2, 128], BF16)
    cmask_h = din("cmask", [128, 4 * 512], BF16)
    epsc_h = din("epsc", [128, 2])
    ropec_h = din("ropec", [128, L])
    ropes_h = din("ropes", [128, L])

    out_h = nc.dram_tensor("out", [B_PER * L, E], F32, kind="ExternalOutput")

    K_ = "ExternalOutput" if debug else "Internal"
    x2_h = nc.dram_tensor("x2scr", [B_PER * L, E], F32, kind=K_)
    wqs_h = nc.dram_tensor("wqscr", [HQ * 128, E], BF16, kind="Internal")
    wos_h = nc.dram_tensor("woscr", [4 * 128, 8192], BF16, kind="Internal")
    wb1_h = nc.dram_tensor("wb1scr", [E, HID], BF16, kind=K_)
    wb2_h = nc.dram_tensor("wb2scr", [HID, E], BF16, kind=K_)
    w1q_h = nc.dram_tensor("w1qscr", [16 * 128, 8192], BF16, kind=K_)
    w2q_h = nc.dram_tensor("w2qscr", [8 * 128, 16384], BF16, kind=K_)
    ssc_h = nc.dram_tensor("sscr", [512, 1024], BF16, kind=K_)
    sbb_h = nc.dram_tensor("sbbscr", [256 * 128, 1024], BF16, kind=K_)
    rhi_h = nc.dram_tensor("rhiscr", [512, 1024], BF16, kind=K_)
    rlo_h = nc.dram_tensor("rloscr", [512, 1024], BF16, kind=K_)

    es = ExitStack()
    with es:
        def sb(name, shape, dt=F32, stack=es):
            return stack.enter_context(nc.sbuf_tensor(name, shape, dt))

        ident = sb("identt", [128, 128], BF16)
        onesrow = sb("onesroww", [1, 128], BF16)
        onescol = sb("onescoll", [128, 1], BF16)
        gmask = sb("gmaskk", [128, 128], BF16)
        indic2 = sb("indic22", [2, 128], BF16)
        epsc = sb("epscc", [128, 2])

        PS = es.enter_context(nc.psum_tensor("PS", [128, 6 * 512], F32))
        PSB = es.enter_context(nc.psum_tensor("PSB", [128, 2 * 1024], BF16))
        ps_rg = [Rg() for _ in range(6)]
        psb_rg = [Rg(), Rg()]

        def bank(i):
            return PS[:, i * 512:(i + 1) * 512]

        p.alloc_sem = lambda name: es.enter_context(nc.semaphore(name))
        for s in ("PE", "DV", "AC"):
            p._sem(s)

        RG = {}

        def rg(name):
            if name not in RG:
                RG[name] = Rg()
            return RG[name]

        for dst, src in ((ident, ident_h), (onesrow, onesrow_h), (onescol, onescol_h),
                         (gmask, gmask_h), (indic2, indic2_h), (epsc, epsc_h)):
            rg(dst.name).write(p, "sync")
            t = p.run("sync", lambda e, d=dst, s=src: e.dma_start(out=d[:], in_=s.ap()), dsem="pre")
            rg(dst.name).noted_write(t)
        p.barrier()

        pc = ExitStack()
        pcb = [sb("pcb%d" % i, [128, 8192], BF16, pc) for i in range(2)]
        pcb_rg = [Rg(), Rg()]
        for i in range(20):
            bi = i % 2
            pcb_rg[bi].write(p, "gpsimd")
            if i < 16:
                t = p.run("gpsimd", lambda e, f=i, b=bi: e.dma_start(
                    out=pcb[b][:, 0:E], in_=bass.AP(wq_h, f * 128 * E, [[E, 128], [1, E]])),
                    dsem="pcl%d" % bi)
            else:
                t = p.run("gpsimd", lambda e, o=(i - 16), b=bi: e.dma_start(
                    out=pcb[b][:], in_=bass.AP(wo_h, o * 128 * 8192, [[8192, 128], [1, 8192]])),
                    dsem="pcl%d" % bi)
            pcb_rg[bi].noted_write(t)
            pcb_rg[bi].read(p, "sync")
            if i < 16:
                t = p.run("sync", lambda e, f=i, b=bi: e.dma_start(
                    out=bass.AP(wqs_h, f * 128 * E, [[E, 128], [1, E]]), in_=pcb[b][:, 0:E]),
                    dsem="pcw%d" % bi)
            else:
                t = p.run("sync", lambda e, o=(i - 16), b=bi: e.dma_start(
                    out=bass.AP(wos_h, o * 128 * 8192, [[8192, 128], [1, 8192]]), in_=pcb[b][:]),
                    dsem="pcw%d" % bi)
            pcb_rg[bi].noted_read(t)
        p.barrier()
        pc.close()

        qs_stack = ExitStack()
        QPAR = 3
        wbb = [sb("wbb%d" % i, [128, 1024], BF16, qs_stack) for i in range(QPAR)]
        ubq = [sb("ubq%d" % i, [128, 1024], BF16, qs_stack) for i in range(QPAR)]
        sbt = [sb("sbt%d" % i, [128, 1024], BF16, qs_stack) for i in range(QPAR)]
        wbb_rg = [Rg() for _ in range(QPAR)]
        ub_rg = [Rg() for _ in range(QPAR)]
        sbt_rg = [Rg() for _ in range(QPAR)]
        wbb2 = [sb("q2w%d" % i, [128, 1024], BF16, qs_stack) for i in range(QPAR)]
        sb2 = [sb("q2s%d" % i, [128, 1024], BF16, qs_stack) for i in range(QPAR)]
        stg_hi = [sb("stg_hi%d" % i, [2, 2 * 1024], BF16, qs_stack) for i in range(2)]
        stg_lo = [sb("stg_lo%d" % i, [2, 2 * 1024], BF16, qs_stack) for i in range(2)]
        tcb = [sb("q2t%d" % i, [128, 1024], BF16, qs_stack) for i in range(QPAR)]
        Abuf = [sb("q2A%d" % i, [128, 1024], BF16, qs_stack) for i in range(QPAR)]
        Bbuf = [sb("q2B%d" % i, [128, 1024], BF16, qs_stack) for i in range(QPAR)]
        qsb = [sb("q2q%d" % i, [128, 1024], BF16, qs_stack) for i in range(QPAR)]
        qpk = sb("qpk", [64, 512], F32, qs_stack)
        qrs = sb("qrs", [64, 512], F32, qs_stack)
        qhi = sb("qhi", [64, 512], BF16, qs_stack)
        qdl = sb("qdl", [64, 512], F32, qs_stack)
        qlo = sb("qlo", [64, 512], BF16, qs_stack)
        w2_rg = [Rg() for _ in range(QPAR)]
        s2_rg = [Rg() for _ in range(QPAR)]
        stg_rg = [[Rg(), Rg()], [Rg(), Rg()]]
        tc_rg = [Rg() for _ in range(QPAR)]
        A_rg = [Rg() for _ in range(QPAR)]
        B_rg = [Rg() for _ in range(QPAR)]
        qs_rg = [Rg() for _ in range(QPAR)]
        qp_rg = Rg()

        def q_tiles():
            ti = 0
            for dc in range(16):
                for hgr in range(8):
                    yield ti, 0, dc, hgr
                    ti += 1
            for hc in range(64):
                for dgr in range(2):
                    yield ti, 1, hc, dgr
                    ti += 1

        def emit_q1(ti, mat, r0, c0):
            bi = ti % QPAR
            src_h = wfcT_h if mat == 0 else wprT_h
            ncols = HID if mat == 0 else E
            wbb_rg[bi].write(p, "gpsimd")
            t = p.run("gpsimd", lambda e, s=src_h, r=r0, c=c0, w=ncols, b=bi:
                      e.dma_start(out=wbb[b][:],
                                  in_=bass.AP(s, r * 128 * w + c * 1024, [[w, 128], [1, 1024]])),
                      dsem="wbl%d" % bi)
            wbb_rg[bi].noted_write(t)
            wbb_rg[bi].read(p, "scalar")
            ub_rg[bi].write(p, "scalar")
            t = p.run("scalar", lambda e, b=bi: e.activation(ubq[b][:], wbb[b][:], AF.Abs))
            wbb_rg[bi].noted_read(t)
            ub_rg[bi].noted_write(t)
            ub_rg[bi].read(p, "tensor")
            rg("gmaskk").read(p, "tensor")
            ps_rg[2 * bi].write(p, "tensor")
            ps_rg[2 * bi + 1].write(p, "tensor")

            def q1mm(e, b=bi):
                e.matmul(bank(2 * b), gmask[:], ubq[b][:, 0:512], start=True, stop=True)
                return e.matmul(bank(2 * b + 1), gmask[:], ubq[b][:, 512:1024], start=True, stop=True)
            t = p.run("tensor", q1mm)
            ps_rg[2 * bi].noted_write(t)
            ps_rg[2 * bi + 1].noted_write(t)
            ub_rg[bi].noted_read(t)
            ps_rg[2 * bi].read(p, "vector")
            ps_rg[2 * bi + 1].read(p, "vector")
            sbt_rg[bi].write(p, "vector")
            t = p.run("vector", lambda e, b=bi: e.tensor_single_scalar(
                sbt[b][:], PS[:, 2 * b * 512: (2 * b + 2) * 512], 1e-8, ALU.max))
            ps_rg[2 * bi].noted_read(t)
            ps_rg[2 * bi + 1].noted_read(t)
            sbt_rg[bi].noted_write(t)
            sbt_rg[bi].write(p, "vector")
            t = p.run("vector", lambda e, b=bi: e.tensor_copy(
                sbt[b][:].bitcast(U16), sbt[b][:].bitcast(U16)))
            sbt_rg[bi].noted_write(t)
            sbt_rg[bi].read(p, "sync")
            t = p.run("sync", lambda e, ti_=ti, b=bi: e.dma_start(
                out=bass.AP(ssc_h, 2 * ti_ * 1024, [[1024, 2], [1, 1024]]),
                in_=bass.AP(sbt[b], 0, [[64 * 1024, 2], [1, 1024]])), dsem="ssw%d" % bi)
            sbt_rg[bi].noted_read(t)
            sbt_rg[bi].read(p, "sync")
            t = p.run("sync", lambda e, ti_=ti, b=bi: e.dma_start(
                out=bass.AP(sbb_h, ti_ * 128 * 1024, [[1024, 128], [1, 1024]]),
                in_=sbt[b][:]), dsem="sbw%d" % bi)
            sbt_rg[bi].noted_read(t)
            wbb_rg[bi].read(p, "sync")
            t = p.run("sync", lambda e, s=(wb1_h if mat == 0 else wb2_h), r=r0, c=c0, w=ncols, b=bi:
                      e.dma_start(out=bass.AP(s, r * 128 * w + c * 1024, [[w, 128], [1, 1024]]),
                                  in_=wbb[b][:]), dsem="wbw%d" % bi)
            wbb_rg[bi].noted_read(t)

        def emit_qp(ch):
            for i_ in range(QPAR):
                p.wait("gpsimd", "ssw%d" % i_, p.counts.get("ssw%d" % i_, 0))
            qp_rg.write(p, "gpsimd")
            t = p.run("gpsimd", lambda e, c=ch: e.dma_start(
                out=qpk[:], in_=bass.AP(ssc_h, c * 32 * 1024, [[512, 64], [1, 512]])), dsem="qpl")
            qp_rg.noted_write(t)
            p.wait("vector", *t)
            t0 = p.run("vector", lambda e: e.tensor_single_scalar(qpk[:], qpk[:], 1e-8, ALU.max))
            p.wait("vector", *t0)
            t1 = p.run("vector", lambda e: e.reciprocal(qrs[:], qpk[:]))
            p.wait("vector", *t1)
            t2 = p.run("vector", lambda e: e.tensor_copy(qhi[:], qrs[:]))
            p.wait("vector", *t2)
            t3 = p.run("vector", lambda e: e.tensor_tensor(qdl[:], qrs[:], qhi[:], ALU.subtract))
            p.wait("vector", *t3)
            tv = p.run("vector", lambda e: e.tensor_copy(qlo[:], qdl[:]))
            qp_rg.noted_write(tv)
            p.wait("sync", *tv)
            t = p.run("sync", lambda e, c=ch: e.dma_start(
                out=bass.AP(rhi_h, c * 32 * 1024, [[512, 64], [1, 512]]), in_=qhi[:]), dsem="qpw")
            qp_rg.noted_read(t)
            t = p.run("sync", lambda e, c=ch: e.dma_start(
                out=bass.AP(rlo_h, c * 32 * 1024, [[512, 64], [1, 512]]), in_=qlo[:]), dsem="qpw")
            qp_rg.noted_read(t)

        def emit_q2(ti, mat, r0, c0):
            if ti == 0:
                sl_ = ["qpw"] + ["sbw%d" % i_ for i_ in range(QPAR)] + ["wbw%d" % i_ for i_ in range(QPAR)]
                for engx in ("sync", "gpsimd"):
                    for s_ in sl_:
                        p.wait(engx, s_, p.counts.get(s_, 0))
            bi = ti % QPAR
            sp = (ti // 2) % 2
            if ti % 2 == 0:
                for si, (stg, srg, sc) in enumerate(((stg_hi, stg_rg[0], rhi_h), (stg_lo, stg_rg[1], rlo_h))):
                    srg[sp].write(p, "sync")
                    t = p.run("sync", lambda e, d=stg[sp], s=sc, ti_=ti: e.dma_start(
                        out=d[:], in_=bass.AP(s, 2 * ti_ * 1024, [[1024, 2], [2048, 2], [1, 1024]])),
                        dsem="stg%d_%d" % (si, sp))
                    srg[sp].noted_write(t)
            off = (ti % 2) * 1024
            src_h = wb1_h if mat == 0 else wb2_h
            ncols = HID if mat == 0 else E
            w2_rg[bi].write(p, "sync")
            t = p.run("sync", lambda e, s=src_h, r=r0, c=c0, w=ncols, b=bi: e.dma_start(
                out=wbb2[b][:], in_=bass.AP(s, r * 128 * w + c * 1024, [[w, 128], [1, 1024]])),
                dsem="w2l%d" % bi)
            w2_rg[bi].noted_write(t)
            s2_rg[bi].write(p, "sync")
            t = p.run("sync", lambda e, ti_=ti, b=bi: e.dma_start(
                out=sb2[b][:], in_=bass.AP(sbb_h, ti_ * 128 * 1024, [[1024, 128], [1, 1024]])),
                dsem="sbl%d" % bi)
            s2_rg[bi].noted_write(t)
            stg_rg[0][sp].read(p, "tensor")
            stg_rg[1][sp].read(p, "tensor")
            ps_rg[2 * bi].write(p, "tensor")
            ps_rg[2 * bi + 1].write(p, "tensor")

            def mmrs(e, o=off, b=bi, s_=sp):
                e.matmul(bank(2 * b), indic2[:], stg_hi[s_][:, o:o + 512], start=True, stop=False)
                e.matmul(bank(2 * b), indic2[:], stg_lo[s_][:, o:o + 512], start=False, stop=True)
                e.matmul(bank(2 * b + 1), indic2[:], stg_hi[s_][:, o + 512:o + 1024], start=True, stop=False)
                return e.matmul(bank(2 * b + 1), indic2[:], stg_lo[s_][:, o + 512:o + 1024], start=False, stop=True)
            t = p.run("tensor", mmrs)
            ps_rg[2 * bi].noted_write(t)
            ps_rg[2 * bi + 1].noted_write(t)
            stg_rg[0][sp].noted_read(t)
            stg_rg[1][sp].noted_read(t)
            w2_rg[bi].read(p, "vector")
            ps_rg[2 * bi].read(p, "vector")
            ps_rg[2 * bi + 1].read(p, "vector")
            tc_rg[bi].write(p, "vector")
            t = p.run("vector", lambda e, b=bi: e.tensor_tensor(
                tcb[b][:], wbb2[b][:], PS[:, 2 * b * 512: (2 * b + 2) * 512], ALU.mult))
            w2_rg[bi].noted_read(t)
            ps_rg[2 * bi].noted_read(t)
            ps_rg[2 * bi + 1].noted_read(t)
            tc_rg[bi].noted_write(t)
            tc_rg[bi].write(p, "vector")
            t = p.run("vector", lambda e, b=bi: e.tensor_copy(
                tcb[b][:].bitcast(U16), tcb[b][:].bitcast(U16)))
            tc_rg[bi].noted_write(t)
            tc_rg[bi].read(p, "vector")
            s2_rg[bi].read(p, "vector")
            A_rg[bi].write(p, "vector")
            t = p.run("vector", lambda e, b=bi: e.scalar_tensor_tensor(
                Abuf[b][:], tcb[b][:], 0.5, sb2[b][:], ALU.is_gt, ALU.mult))
            A_rg[bi].noted_write(t)
            B_rg[bi].write(p, "vector")
            t = p.run("vector", lambda e, b=bi: e.scalar_tensor_tensor(
                Bbuf[b][:], tcb[b][:], -0.5, sb2[b][:], ALU.is_lt, ALU.mult))
            tc_rg[bi].noted_read(t)
            s2_rg[bi].noted_read(t)
            B_rg[bi].noted_write(t)
            A_rg[bi].read(p, "vector")
            B_rg[bi].read(p, "vector")
            qs_rg[bi].write(p, "vector")
            t = p.run("vector", lambda e, b=bi: e.tensor_tensor(qsb[b][:], Abuf[b][:], Bbuf[b][:], ALU.subtract))
            A_rg[bi].noted_read(t)
            B_rg[bi].noted_read(t)
            qs_rg[bi].noted_write(t)
            qs_rg[bi].read(p, "gpsimd")
            if mat == 0:
                t = p.run("gpsimd", lambda e, o=(2 * c0) * 128 * 8192 + r0 * 512, b=bi:
                          e.dma_start(out=bass.AP(w1q_h, o, [[8192, 128], [128 * 8192, 2], [1, 512]]),
                                      in_=qsb[b][:]), dsem="qqw%d" % bi)
            else:
                t = p.run("gpsimd", lambda e, o=(4 * c0) * 128 * 16384 + r0 * 256, b=bi:
                          e.dma_start(out=bass.AP(w2q_h, o, [[16384, 128], [128 * 16384, 4], [1, 256]]),
                                      in_=qsb[b][:]), dsem="qqw%d" % bi)
            qs_rg[bi].noted_read(t)

        import functools
        quant_units = [functools.partial(emit_q1, *a) for a in q_tiles()]
        quant_units += [functools.partial(emit_qp, ch) for ch in range(16)]
        quant_units += [functools.partial(emit_q2, *a) for a in q_tiles()]
        quant_units.reverse()

        def drain_quant(k=1):
            for _ in range(k):
                if quant_units:
                    quant_units.pop()()

        while quant_units:
            drain_quant(8)
        p.barrier()
        qs_stack.close()

        ab = ExitStack()
        cmask = sb("cmaskk", [128, 4 * 512], BF16, ab)
        ropec = sb("ropecc", [128, L], F32, ab)
        ropes = sb("ropess", [128, L], F32, ab)
        bq_t = sb("bq_tt", [128, 16], F32, ab)
        bk_t = sb("bk_tt", [128, 2], F32, ab)
        bvrow = sb("bvroww", [1, 256], F32, ab)
        bvh = sb("bvhh", [1, 256], BF16, ab)
        bvl = sb("bvll", [1, 256], BF16, ab)
        vbias = sb("vbiass", [128, 256], F32, ab)
        wkb = sb("wkbb", [128, 2 * E], BF16, ab)
        wvb = sb("wvbb", [128, 16 * 256], BF16, ab)
        for dst_, src_ in ((cmask, cmask_h), (bq_t, bq_h), (bk_t, bk_h)):
            rg(dst_.name).write(p, "sync")
            t = p.run("sync", lambda e, d=dst_, s=src_: e.dma_start(out=d[:], in_=s.ap()), dsem="pre2")
            rg(dst_.name).noted_write(t)
        for dst_, src_ in ((wvb, wv_h),):
            rg(dst_.name).write(p, "gpsimd")
            t = p.run("gpsimd", lambda e, d=dst_, s=src_: e.dma_start(out=d[:], in_=s.ap()), dsem="preg2")
            rg(dst_.name).noted_write(t)
        for dst_, src_ in ((ropec, ropec_h), (ropes, ropes_h), (bvrow, bv_h)):
            rg(dst_.name).write(p, "sync")
            t = p.run("sync", lambda e, d=dst_, s=src_: e.dma_start(out=d[:], in_=s.ap()), dsem="pre2")
            rg(dst_.name).noted_write(t)
        rg("wkbb").write(p, "gpsimd")
        t = p.run("gpsimd", lambda e: e.dma_start(
            out=wkb[:], in_=bass.AP(wk_h, 0, [[E, 128], [128 * E, 2], [1, E]])), dsem="preg2")
        rg("wkbb").noted_write(t)
        p.barrier()
        rg("bvroww").read(p, "vector")
        rg("bvhh").write(p, "vector")
        t = p.run("vector", lambda e: e.tensor_copy(bvh[:], bvrow[:]))
        rg("bvhh").noted_write(t)
        rg("bvll").write(p, "vector")
        rg("bvhh").read(p, "vector")
        t = p.run("vector", lambda e: e.tensor_tensor(bvl[:], bvrow[:], bvh[:], ALU.subtract))
        rg("bvroww").noted_read(t)
        rg("bvhh").noted_read(t)
        rg("bvll").noted_write(t)
        rg("bvhh").read(p, "tensor")
        rg("bvll").read(p, "tensor")
        rg("onesroww").read(p, "tensor")
        ps_rg[0].write(p, "tensor")

        def bvmm(e):
            e.matmul(PS[:, 0:256], onesrow[:], bvh[:], start=True, stop=False)
            return e.matmul(PS[:, 0:256], onesrow[:], bvl[:], start=False, stop=True)
        t = p.run("tensor", bvmm)
        ps_rg[0].noted_write(t)
        ps_rg[0].read(p, "scalar")
        rg("vbiass").write(p, "scalar")
        t = p.run("scalar", lambda e: e.activation(vbias[:], PS[:, 0:256], AF.Copy))
        ps_rg[0].noted_read(t)
        rg("vbiass").noted_write(t)
        xa = sb("xa", [128, E], F32, ab)
        trash = [sb("trasha0", [128, E], BF16, ab), sb("trasha1", [128, E], BF16, ab)]
        stat = [sb("stata0", [128, 8], F32, ab), sb("stata1", [128, 8], F32, ab)]
        xln1t = sb("xln1t", [128, 16 * 512], BF16, ab)
        wqb = [sb("wqb0", [128, E], BF16, ab), sb("wqb1", [128, E], BF16, ab)]
        qevac = [sb("qevac0", [128, 512], BF16, ab), sb("qevac1", [128, 512], BF16, ab)]
        qswap = [sb("qswap0", [128, 512], BF16, ab), sb("qswap1", [128, 512], BF16, ab)]
        ropem = [sb("ropem0", [128, 512], F32, ab), sb("ropem1", [128, 512], F32, ab)]
        qtq = sb("qtq", [128, 16 * 512], BF16, ab)
        ktb = sb("ktb", [128, 2 * L], BF16, ab)
        vtb = sb("vtb", [128, 16 * 256], BF16, ab)
        ptb = [sb("ptb0", [128, 512], BF16, ab), sb("ptb1", [128, 512], BF16, ab)]
        srow = sb("srow", [1, 512], F32, ab)
        rhirow = sb("rhirow", [1, 512], BF16, ab)
        rlorow = sb("rlorow", [1, 512], BF16, ab)
        rbs = [sb("rbs0", [128, 512], F32, ab), sb("rbs1", [128, 512], F32, ab)]
        avt = sb("avt", [128, 16 * 512], BF16, ab)
        wob = [sb("wob0", [128, 16 * 512], BF16, ab)]
        xr = [sb("xr0", [128, 512], F32, ab), sb("xr1", [128, 512], F32, ab)]
        x2st = [sb("x2st0", [128, 512], F32, ab), sb("x2st1", [128, 512], F32, ab)]

        xa_rg = Rg(); xln_rg = Rg(); wq_rg = [Rg(), Rg()]; qe_rg = [Rg(), Rg()]
        qt_rg = Rg(); kt_rg = Rg(); vt_rg = Rg()
        pt_rg = [Rg(), Rg()]; row_rg = Rg(); rbs_rg = [Rg(), Rg()]; avt_rg = Rg()
        wo_rg = [Rg(), Rg()]; xr_rg = [Rg(), Rg()]; x2_rg = [Rg(), Rg()]
        stat_rg = [Rg(), Rg()]; trash_rg = [Rg(), Rg()]

        def ln_into(xsrc_h, row0, xabuf, xarg, trbuf, trrg, stbuf, strg, xsem):
            xarg.write(p, "sync")
            t = p.run("sync", lambda e: e.dma_start(
                out=xabuf[:], in_=bass.AP(xsrc_h, row0 * E, [[E, 128], [1, E]])), dsem=xsem)
            xarg.noted_write(t)
            xarg.read(p, "vector")
            strg.write(p, "vector")
            t = p.run("vector", lambda e: e.memset(stbuf[:, 1:2], 0.0))
            strg.noted_write(t)
            strg.write(p, "vector")
            xarg.read(p, "vector")
            t = p.run("vector", lambda e: e.reduce_sum(stbuf[:, 0:1], xabuf[:], axis=AX.X))
            xarg.noted_read(t)
            strg.noted_write(t)
            xarg.read(p, "scalar")
            strg.write(p, "scalar")
            trrg.write(p, "scalar")
            t = p.run("scalar", lambda e: e.activation(
                trbuf[:], xabuf[:], AF.Square, accum_out=stbuf[:, 1:2]))
            xarg.noted_read(t)
            trrg.noted_write(t)
            strg.noted_write(t)
            for fn in (
                lambda e: e.tensor_scalar(stbuf[:, 2:3], stbuf[:, 0:1], 1.0 / E, None, ALU.mult),
                lambda e: e.tensor_scalar(stbuf[:, 3:4], stbuf[:, 1:2], 1.0 / E, None, ALU.mult),
                lambda e: e.scalar_tensor_tensor(stbuf[:, 4:5], stbuf[:, 2:3], -1.0, stbuf[:, 2:3],
                                                 ALU.mult, ALU.mult),
                lambda e: e.tensor_tensor(stbuf[:, 4:5], stbuf[:, 3:4], stbuf[:, 4:5], ALU.add),
            ):
                strg.write(p, "vector")
                t = p.run("vector", fn)
                strg.noted_write(t)
            strg.write(p, "scalar")
            t = p.run("scalar", lambda e: e.activation(stbuf[:, 5:6], stbuf[:, 4:5], AF.Ln, bias=epsc[:, 0:1]))
            strg.noted_write(t)
            strg.write(p, "scalar")
            t = p.run("scalar", lambda e: e.activation(stbuf[:, 6:7], stbuf[:, 5:6], AF.Exp, scale=-0.5))
            strg.noted_write(t)
            strg.write(p, "vector")
            t = p.run("vector", lambda e: e.scalar_tensor_tensor(
                stbuf[:, 7:8], stbuf[:, 2:3], -1.0, stbuf[:, 6:7], ALU.mult, ALU.mult))
            strg.noted_write(t)
            xarg.read(p, "scalar")
            strg.read(p, "scalar")
            trrg.write(p, "scalar")
            t = p.run("scalar", lambda e: e.activation(
                trbuf[:], xabuf[:], AF.Identity, bias=stbuf[:, 7:8], scale=stbuf[:, 6:7]))
            xarg.noted_read(t)
            strg.noted_read(t)
            trrg.noted_write(t)

        def transpose_into(tsub, trbuf, trrg, dstbuf, dstrg):
            for grp in range(2):
                trrg.read(p, "tensor")
                psb_rg[grp].write(p, "tensor")

                def tp(e, g=grp):
                    for j in range(8):
                        c = g * 8 + j
                        ins = e.transpose(PSB[:, g * 1024 + j * 128: g * 1024 + (j + 1) * 128],
                                          trbuf[:, c * 128:(c + 1) * 128], ident[:])
                    return ins
                t = p.run("tensor", tp)
                psb_rg[grp].noted_write(t)
                trrg.noted_read(t)
                psb_rg[grp].read(p, "vector")
                dstrg.write(p, "vector")

                def ev(e, g=grp, ts=tsub):
                    return e.tensor_copy(
                        dstbuf[:].rearrange("p (dc t) -> p dc t", t=512)
                        [:, g * 8:(g + 1) * 8, ts * 128:(ts + 1) * 128],
                        PSB[:, g * 1024:(g + 1) * 1024].rearrange("p (j t) -> p j t", t=128))
                t = p.run("vector", ev)
                psb_rg[grp].noted_read(t)
                dstrg.noted_write(t)

        qsw_rg = [Rg(), Rg()]
        rm_rg = [Rg(), Rg()]

        def rope_into(qt_, dstbuf, dstrg, d0, par):
            qe = qevac[par]
            qs_ = qswap[par]
            qe_rg[par].read(p, "gpsimd")
            qsw_rg[par].write(p, "gpsimd")
            t = p.run("gpsimd", lambda e: e.dma_start(out=qs_[0:64, :], in_=qe[64:128, :]), dsem="swp%d" % par)
            qsw_rg[par].noted_write(t)
            t = p.run("gpsimd", lambda e: e.dma_start(out=qs_[64:128, :], in_=qe[0:64, :]), dsem="swp%d" % par)
            qsw_rg[par].noted_write(t)
            qe_rg[par].noted_read(t)
            qe_rg[par].read(p, "vector")
            qsw_rg[par].read(p, "vector")
            rg("ropecc").read(p, "vector")
            rg("ropess").read(p, "vector")
            dstrg.write(p, "vector")
            qt__, d0_ = qt_, d0
            csl = ropec[0:64, qt__ * 512:(qt__ + 1) * 512]
            snl = ropes[0:64, qt__ * 512:(qt__ + 1) * 512]
            csh = ropec[64:128, qt__ * 512:(qt__ + 1) * 512]
            snh = ropes[64:128, qt__ * 512:(qt__ + 1) * 512]
            rm_rg[0].write(p, "vector")
            t = p.run("vector", lambda e, a=csl, q_=qe: e.tensor_tensor(ropem[0][0:64, :], q_[0:64, :], a, ALU.mult))
            rm_rg[0].noted_write(t)
            rm_rg[1].write(p, "vector")
            t = p.run("vector", lambda e, a=snl, q_=qs_: e.tensor_tensor(ropem[1][0:64, :], q_[0:64, :], a, ALU.mult))
            rm_rg[1].noted_write(t)
            rm_rg[0].read(p, "vector")
            rm_rg[1].read(p, "vector")
            dstrg.write(p, "vector")
            t = p.run("vector", lambda e, d=d0_, db=dstbuf: e.tensor_tensor(
                db[0:64, d:d + 512], ropem[0][0:64, :], ropem[1][0:64, :], ALU.subtract))
            rm_rg[0].noted_read(t)
            rm_rg[1].noted_read(t)
            dstrg.noted_write(t)
            rm_rg[0].write(p, "vector")
            qe_rg[par].read(p, "vector")
            t = p.run("vector", lambda e, a=csh, q_=qe: e.tensor_tensor(ropem[0][64:128, :], q_[64:128, :], a, ALU.mult))
            rm_rg[0].noted_write(t)
            qe_rg[par].noted_read(t)
            rm_rg[1].write(p, "vector")
            qsw_rg[par].read(p, "vector")
            t = p.run("vector", lambda e, a=snh, q_=qs_: e.tensor_tensor(ropem[1][64:128, :], q_[64:128, :], a, ALU.mult))
            rm_rg[1].noted_write(t)
            qsw_rg[par].noted_read(t)
            rm_rg[0].read(p, "vector")
            rm_rg[1].read(p, "vector")
            dstrg.write(p, "vector")
            t = p.run("vector", lambda e, d=d0_, db=dstbuf: e.tensor_tensor(
                db[64:128, d:d + 512], ropem[0][64:128, :], ropem[1][64:128, :], ALU.add))
            rm_rg[0].noted_read(t)
            rm_rg[1].noted_read(t)
            dstrg.noted_write(t)

        for b in range(B_PER):
            for qt in range(4):
                nkc = 4 * (qt + 1)
                for tsub in range(4):
                    row0 = b * L + qt * 512 + tsub * 128
                    tp_ = tsub % 2
                    ln_into(x_h, row0, xa, xa_rg, trash[tp_], trash_rg[tp_], stat[tp_], stat_rg[tp_], "xal")
                    transpose_into(tsub, trash[tp_], trash_rg[tp_], xln1t, xln_rg)
                    drain_quant(2)
                for fc in range(HQ):
                    bi = fc % 2
                    wq_rg[bi].write(p, "sync")
                    t = p.run("sync", lambda e, f=fc, b_=bi: e.dma_start(
                        out=wqb[b_][:], in_=bass.AP(wqs_h, f * 128 * E, [[E, 128], [1, E]])),
                        dsem="wql%d" % bi)
                    wq_rg[bi].noted_write(t)
                    pb = fc % 2
                    wq_rg[bi].read(p, "tensor")
                    xln_rg.read(p, "tensor")
                    ps_rg[pb].write(p, "tensor")

                    def qmm(e, b_=bi, pb_=pb):
                        for dc in range(16):
                            ins = e.matmul(bank(pb_), wqb[b_][:, dc * 128:(dc + 1) * 128],
                                           xln1t[:, dc * 512:(dc + 1) * 512],
                                           start=(dc == 0), stop=(dc == 15))
                        return ins
                    t = p.run("tensor", qmm)
                    ps_rg[pb].noted_write(t)
                    wq_rg[bi].noted_read(t)
                    xln_rg.noted_read(t)
                    ps_rg[pb].read(p, "scalar")
                    rg("bq_tt").read(p, "scalar")
                    qe_rg[bi].write(p, "scalar")
                    t = p.run("scalar", lambda e, pb_=pb, f=fc, b_=bi: e.activation(
                        qevac[b_][:], bank(pb_), AF.Identity, bias=bq_t[:, f:f + 1]))
                    ps_rg[pb].noted_read(t)
                    qe_rg[bi].noted_write(t)
                    rope_into(qt, qtq, qt_rg, fc * 512, bi)
                    drain_quant(2)
                for kv in range(HKV):
                    pb = kv % 2
                    rg("wkbb").read(p, "tensor")
                    xln_rg.read(p, "tensor")
                    ps_rg[pb].write(p, "tensor")

                    def kmm(e, k=kv, pb_=pb):
                        for dc in range(16):
                            ins = e.matmul(bank(pb_), wkb[:, k * E + dc * 128: k * E + (dc + 1) * 128],
                                           xln1t[:, dc * 512:(dc + 1) * 512],
                                           start=(dc == 0), stop=(dc == 15))
                        return ins
                    t = p.run("tensor", kmm)
                    ps_rg[pb].noted_write(t)
                    xln_rg.noted_read(t)
                    ps_rg[pb].read(p, "scalar")
                    rg("bk_tt").read(p, "scalar")
                    qe_rg[kv].write(p, "scalar")
                    t = p.run("scalar", lambda e, pb_=pb, k=kv: e.activation(
                        qevac[k][:], bank(pb_), AF.Identity, bias=bk_t[:, k:k + 1]))
                    ps_rg[pb].noted_read(t)
                    qe_rg[kv].noted_write(t)
                    rope_into(qt, ktb, kt_rg, kv * L + qt * 512, kv)
                    drain_quant(1)
                for tsub in range(4):
                    pb = 2
                    rg("wvbb").read(p, "tensor")
                    xln_rg.read(p, "tensor")
                    ps_rg[pb].write(p, "tensor")

                    def vmm(e, ts=tsub, pb_=pb):
                        for dc in range(16):
                            ins = e.matmul(PS[:, pb_ * 512: pb_ * 512 + 256],
                                           xln1t[:, dc * 512 + ts * 128: dc * 512 + (ts + 1) * 128],
                                           wvb[:, dc * 256:(dc + 1) * 256],
                                           start=(dc == 0), stop=(dc == 15))
                        return ins
                    t = p.run("tensor", vmm)
                    ps_rg[pb].noted_write(t)
                    xln_rg.noted_read(t)
                    ps_rg[pb].read(p, "vector")
                    rg("vbiass").read(p, "vector")
                    vt_rg.write(p, "vector")
                    t = p.run("vector", lambda e, pb_=pb, ts=tsub, q_=qt: e.tensor_tensor(
                        vtb[:, (q_ * 4 + ts) * 256:(q_ * 4 + ts + 1) * 256],
                        PS[:, pb_ * 512: pb_ * 512 + 256], vbias[:], ALU.add))
                    ps_rg[pb].noted_read(t)
                    vt_rg.noted_write(t)
                    drain_quant(1)
                pending_tail = [None]
                for kv in range(HKV):
                    for g in range(8):
                        h = kv * 8 + g
                        hp = h % 2
                        avb = 2 if hp == 0 else 4
                        smb = 3 if hp == 0 else 5
                        def emit_S(kc):
                            o = kc - 4 * qt
                            sbk = kc % 2
                            kt_rg.read(p, "tensor")
                            qt_rg.read(p, "tensor")
                            ps_rg[sbk].write(p, "tensor")

                            def smm(e, k=kv, h_=h, kc_=kc, o_=o, sbk_=sbk):
                                ins = e.matmul(bank(sbk_),
                                               ktb[:, k * L + kc_ * 128: k * L + (kc_ + 1) * 128],
                                               qtq[:, h_ * 512:(h_ + 1) * 512],
                                               start=True, stop=(o_ < 0))
                                if o_ >= 0:
                                    ins = e.matmul(bank(sbk_), ident[:],
                                                   cmask[:, o_ * 512:(o_ + 1) * 512],
                                                   start=False, stop=True)
                                return ins
                            t = p.run("tensor", smm)
                            ps_rg[sbk].noted_write(t)
                            kt_rg.noted_read(t)
                            qt_rg.noted_read(t)

                        emit_S(0)
                        for kc in range(nkc):
                            sbk = kc % 2
                            ps_rg[sbk].read(p, "scalar")
                            pt_rg[sbk].write(p, "scalar")
                            t = p.run("scalar", lambda e, sbk_=sbk: e.activation(
                                ptb[sbk_][:], bank(sbk_), AF.Exp, scale=SCALE))
                            ps_rg[sbk].noted_read(t)
                            pt_rg[sbk].noted_write(t)
                            if kc + 1 < nkc:
                                emit_S(kc + 1)
                            if kc == 1 and pending_tail[0] is not None:
                                pending_tail[0]()
                                pending_tail[0] = None
                            pt_rg[sbk].read(p, "tensor")
                            vt_rg.read(p, "tensor")
                            ps_rg[avb].write(p, "tensor")
                            ps_rg[smb].write(p, "tensor")

                            def avmm(e, k=kv, kc_=kc, sbk_=sbk, first=(kc == 0), last=(kc == nkc - 1),
                                     avb_=avb, smb_=smb):
                                e.matmul(bank(avb_),
                                         vtb[:, kc_ * 256 + k * 128: kc_ * 256 + k * 128 + 128],
                                         ptb[sbk_][:], start=first, stop=last)
                                return e.matmul(PS[0:1, smb_ * 512:(smb_ + 1) * 512], onescol[:], ptb[sbk_][:],
                                                start=first, stop=last)
                            t = p.run("tensor", avmm)
                            ps_rg[avb].noted_write(t)
                            ps_rg[smb].noted_write(t)
                            pt_rg[sbk].noted_read(t)
                            vt_rg.noted_read(t)
                        def tail(h=h, hp=hp, avb=avb, smb=smb):
                            ps_rg[smb].read(p, "scalar")
                            row_rg.write(p, "scalar")
                            t = p.run("scalar", lambda e, smb_=smb: e.activation(
                                srow[:], PS[0:1, smb_ * 512:(smb_ + 1) * 512], AF.Copy))
                            ps_rg[smb].noted_read(t)
                            row_rg.noted_write(t)
                            for fn in (lambda e: e.activation(srow[:], srow[:], AF.Ln),
                                       lambda e: e.activation(srow[:], srow[:], AF.Exp, scale=-1.0),
                                       lambda e: e.activation(rhirow[:], srow[:], AF.Copy)):
                                row_rg.write(p, "scalar")
                                t = p.run("scalar", fn)
                                row_rg.noted_write(t)
                            row_rg.write(p, "vector")
                            t = p.run("vector", lambda e: e.tensor_tensor(
                                rlorow[:], srow[:], rhirow[:], ALU.subtract))
                            row_rg.noted_write(t)
                            row_rg.read(p, "tensor")
                            rg("onesroww").read(p, "tensor")
                            ps_rg[smb].write(p, "tensor")

                            def rbmm(e, smb_=smb):
                                e.matmul(bank(smb_), onesrow[:], rhirow[:], start=True, stop=False)
                                return e.matmul(bank(smb_), onesrow[:], rlorow[:], start=False, stop=True)
                            t = p.run("tensor", rbmm)
                            ps_rg[smb].noted_write(t)
                            row_rg.noted_read(t)
                            ps_rg[smb].read(p, "scalar")
                            rbs_rg[hp].write(p, "scalar")
                            t = p.run("scalar", lambda e, smb_=smb, hp_=hp: e.activation(
                                rbs[hp_][:], bank(smb_), AF.Copy))
                            ps_rg[smb].noted_read(t)
                            rbs_rg[hp].noted_write(t)
                            ps_rg[avb].read(p, "vector")
                            rbs_rg[hp].read(p, "vector")
                            avt_rg.write(p, "vector")
                            t = p.run("vector", lambda e, h_=h, avb_=avb, hp_=hp: e.tensor_tensor(
                                avt[:, h_ * 512:(h_ + 1) * 512], bank(avb_), rbs[hp_][:], ALU.mult))
                            ps_rg[avb].noted_read(t)
                            rbs_rg[hp].noted_read(t)
                            avt_rg.noted_write(t)
                            drain_quant(2)
                        pending_tail[0] = tail
                if pending_tail[0] is not None:
                    pending_tail[0]()
                    pending_tail[0] = None
                for oq in range(4):
                    wp = 0
                    wo_rg[wp].write(p, "sync")
                    t = p.run("sync", lambda e, o=oq, w_=wp: e.dma_start(
                        out=wob[w_][:], in_=bass.AP(wos_h, o * 128 * 8192,
                                                    [[8192, 128], [1, 8192]])), dsem="wol%d" % wp)
                    wo_rg[wp].noted_write(t)
                    for tsub in range(4):
                        sl = tsub % 2
                        r0 = b * L + qt * 512 + tsub * 128
                        xr_rg[sl].write(p, "sync")
                        t = p.run("sync", lambda e, r_=r0, o=oq, sl_=sl: e.dma_start(
                            out=xr[sl_][:], in_=bass.AP(x_h, r_ * E + o * 512, [[E, 128], [1, 512]])),
                            dsem="xrl%d" % sl)
                        xr_rg[sl].noted_write(t)
                        pb = tsub % 2
                        avt_rg.read(p, "tensor")
                        wo_rg[wp].read(p, "tensor")
                        ps_rg[pb].write(p, "tensor")

                        def omm(e, ts=tsub, pb_=pb, w_=wp):
                            for dc in range(16):
                                ins = e.matmul(bank(pb_),
                                               avt[:, dc * 512 + ts * 128: dc * 512 + (ts + 1) * 128],
                                               wob[w_][:, dc * 512:(dc + 1) * 512],
                                               start=(dc == 0), stop=(dc == 15))
                            return ins
                        t = p.run("tensor", omm)
                        ps_rg[pb].noted_write(t)
                        avt_rg.noted_read(t)
                        wo_rg[wp].noted_read(t)
                        ps_rg[pb].read(p, "vector")
                        xr_rg[sl].read(p, "vector")
                        x2_rg[sl].write(p, "vector")
                        t = p.run("vector", lambda e, pb_=pb, sl_=sl: e.tensor_tensor(
                            x2st[sl_][:], bank(pb_), xr[sl_][:], ALU.add))
                        ps_rg[pb].noted_read(t)
                        xr_rg[sl].noted_read(t)
                        x2_rg[sl].noted_write(t)
                        x2_rg[sl].read(p, "gpsimd")
                        t = p.run("gpsimd", lambda e, r_=r0, o=oq, sl_=sl: e.dma_start(
                            out=bass.AP(x2_h, r_ * E + o * 512, [[E, 128], [1, 512]]),
                            in_=x2st[sl_][:]), dsem="x2w%d" % sl)
                        x2_rg[sl].noted_read(t)
                    drain_quant(1)
        p.barrier()
        ab.close()

        cc = ExitStack()
        xc = sb("xc", [128, E], F32, cc)
        trc = [sb("trc0", [128, E], BF16, cc), sb("trc1", [128, E], BF16, cc)]
        stc = [sb("stc0", [128, 8], F32, cc), sb("stc1", [128, 8], F32, cc)]
        xln2t = sb("xln2t", [128, 16 * 512], BF16, cc)
        w1b = [sb("w1b0", [128, 16 * 512], BF16, cc), sb("w1b1", [128, 16 * 512], BF16, cc)]
        relf = sb("relf", [128, 512], BF16, cc)
        h2t = sb("h2t", [128, 64 * 512], BF16, cc)
        dtmp = sb("dtmp", [128, 128], F32, cc)
        invb = sb("invb", [128, 8], F32, cc)
        w2b = [sb("w2b0", [128, 64 * 256], BF16, cc), sb("w2b1", [128, 64 * 256], BF16, cc)]
        x2r = [sb("x2r0", [128, 256], F32, cc), sb("x2r1", [128, 256], F32, cc)]
        outst = [sb("outst0", [128, 256], F32, cc), sb("outst1", [128, 256], F32, cc)]

        xc_rg = Rg(); trc_rg = [Rg(), Rg()]; stc_rg = [Rg(), Rg()]; xl2_rg = Rg(); dt_rg = Rg()
        w1_rg = [Rg(), Rg()]; rel_rg = Rg(); h2_rg = Rg()
        inv_rg = Rg(); w2q_rg = [Rg(), Rg()]; x2r_rg = [Rg(), Rg()]; out_rg = [Rg(), Rg()]

        for tile in range(8):
            base = tile * 512
            for tsub in range(4):
                tp_ = tsub % 2
                ln_into(x2_h, base + tsub * 128, xc, xc_rg, trc[tp_], trc_rg[tp_], stc[tp_], stc_rg[tp_], "xcl")
                transpose_into(tsub, trc[tp_], trc_rg[tp_], xln2t, xl2_rg)
            for hg in range(16):
                bi = hg % 2
                w1_rg[bi].write(p, "gpsimd")
                t = p.run("gpsimd", lambda e, h=hg, b_=bi: e.dma_start(
                    out=w1b[b_][:],
                    in_=bass.AP(w1q_h, h * 128 * 8192, [[8192, 128], [1, 8192]])), dsem="w1l%d" % bi)
                w1_rg[bi].noted_write(t)
                for hc4 in range(4):
                    hc = hg * 4 + hc4
                    pb = hc % 2
                    w1_rg[bi].read(p, "tensor")
                    xl2_rg.read(p, "tensor")
                    ps_rg[pb].write(p, "tensor")

                    def f1mm(e, b_=bi, h4=hc4, pb_=pb):
                        for dc in range(16):
                            ins = e.matmul(bank(pb_),
                                           w1b[b_][:, dc * 512 + h4 * 128: dc * 512 + (h4 + 1) * 128],
                                           xln2t[:, dc * 512:(dc + 1) * 512],
                                           start=(dc == 0), stop=(dc == 15))
                        return ins
                    t = p.run("tensor", f1mm)
                    ps_rg[pb].noted_write(t)
                    w1_rg[bi].noted_read(t)
                    xl2_rg.noted_read(t)
                    ps_rg[pb].read(p, "scalar")
                    rel_rg.write(p, "scalar")
                    t = p.run("scalar", lambda e, pb_=pb: e.activation(relf[:], bank(pb_), AF.Relu))
                    ps_rg[pb].noted_read(t)
                    rel_rg.noted_write(t)
                    rel_rg.read(p, "scalar")
                    h2_rg.write(p, "scalar")
                    t = p.run("scalar", lambda e, h_=hc: e.activation(
                        h2t[:, h_ * 512:(h_ + 1) * 512], relf[:], AF.Square))
                    rel_rg.noted_read(t)
                    h2_rg.noted_write(t)
                h2_rg.read(p, "tensor")
                ps_rg[2].write(p, "tensor")

                def dmm(e, hg_=hg):
                    for hc4 in range(4):
                        h_ = hg_ * 4 + hc4
                        for ts in range(4):
                            ins = e.matmul(PS[:, 2 * 512 + ts * 128: 2 * 512 + (ts + 1) * 128],
                                           h2t[:, h_ * 512 + ts * 128: h_ * 512 + (ts + 1) * 128],
                                           h2t[:, h_ * 512 + ts * 128: h_ * 512 + (ts + 1) * 128],
                                           start=(h_ == 0 and ts == 0), stop=(h_ == 63 and ts == 3))
                    return ins
                t = p.run("tensor", dmm)
                ps_rg[2].noted_write(t)
                h2_rg.noted_read(t)
            for tsub in range(4):
                ps_rg[2].read(p, "vector")
                dt_rg.write(p, "vector")
                t = p.run("vector", lambda e, ts=tsub: e.tensor_tensor(
                    dtmp[:], PS[:, 2 * 512 + ts * 128: 2 * 512 + (ts + 1) * 128], ident[:], ALU.mult))
                ps_rg[2].noted_read(t)
                dt_rg.noted_write(t)
                dt_rg.read(p, "vector")
                inv_rg.write(p, "vector")
                t = p.run("vector", lambda e, ts=tsub: e.reduce_sum(invb[:, ts:ts + 1], dtmp[:], axis=AX.X))
                dt_rg.noted_read(t)
                inv_rg.noted_write(t)
                inv_rg.write(p, "vector")
                t = p.run("vector", lambda e, ts=tsub: e.tensor_scalar(
                    invb[:, 4 + ts:5 + ts], invb[:, ts:ts + 1], 1.0 / HID, None, ALU.mult))
                inv_rg.noted_write(t)
                inv_rg.write(p, "scalar")
                t = p.run("scalar", lambda e, ts=tsub: e.activation(
                    invb[:, ts:ts + 1], invb[:, 4 + ts:5 + ts], AF.Ln, bias=epsc[:, 1:2]))
                inv_rg.noted_write(t)
                inv_rg.write(p, "scalar")
                t = p.run("scalar", lambda e, ts=tsub: e.activation(
                    invb[:, 4 + ts:5 + ts], invb[:, ts:ts + 1], AF.Exp, scale=-0.5))
                inv_rg.noted_write(t)
            for dq in range(8):
                wb_ = dq % 2
                w2q_rg[wb_].write(p, "gpsimd")
                t = p.run("gpsimd", lambda e, d=dq, b_=wb_: e.dma_start(
                    out=w2b[b_][:],
                    in_=bass.AP(w2q_h, d * 128 * 16384, [[16384, 128], [1, 16384]])),
                    dsem="w2s%d" % wb_)
                w2q_rg[wb_].noted_write(t)
                for tsub in range(4):
                    pb = 3 + (tsub % 2)
                    sl = tsub % 2
                    h2_rg.read(p, "tensor")
                    w2q_rg[wb_].read(p, "tensor")
                    ps_rg[pb].write(p, "tensor")

                    def f2mm(e, ts=tsub, pb_=pb, b_=wb_):
                        for hc in range(64):
                            ins = e.matmul(PS[:, pb_ * 512: pb_ * 512 + 256],
                                           h2t[:, hc * 512 + ts * 128: hc * 512 + (ts + 1) * 128],
                                           w2b[b_][:, hc * 256:(hc + 1) * 256],
                                           start=(hc == 0), stop=(hc == 63))
                        return ins
                    t = p.run("tensor", f2mm)
                    ps_rg[pb].noted_write(t)
                    h2_rg.noted_read(t)
                    w2q_rg[wb_].noted_read(t)
                    x2r_rg[sl].write(p, "sync")
                    t = p.run("sync", lambda e, ts=tsub, d=dq, b_=base, sl_=sl: e.dma_start(
                        out=x2r[sl_][:],
                        in_=bass.AP(x2_h, (b_ + ts * 128) * E + d * 256, [[E, 128], [1, 256]])),
                        dsem="x2r%d" % sl)
                    x2r_rg[sl].noted_write(t)
                    ps_rg[pb].read(p, "vector")
                    x2r_rg[sl].read(p, "vector")
                    inv_rg.read(p, "vector")
                    out_rg[sl].write(p, "vector")
                    t = p.run("vector", lambda e, ts=tsub, pb_=pb, sl_=sl: e.scalar_tensor_tensor(
                        outst[sl_][:], PS[:, pb_ * 512: pb_ * 512 + 256],
                        invb[:, 4 + ts:5 + ts], x2r[sl_][:], ALU.mult, ALU.add))
                    ps_rg[pb].noted_read(t)
                    x2r_rg[sl].noted_read(t)
                    inv_rg.noted_read(t)
                    out_rg[sl].noted_write(t)
                    out_rg[sl].read(p, "sync")
                    t = p.run("sync", lambda e, ts=tsub, d=dq, b_=base, sl_=sl: e.dma_start(
                        out=bass.AP(out_h, (b_ + ts * 128) * E + d * 256, [[E, 128], [1, 256]]),
                        in_=outst[sl_][:]), dsem="osw%d" % sl)
                    out_rg[sl].noted_read(t)
        cc.close()

        p.barrier()

        with nc.Block() as block:
            @block.sync
            def _(e):
                for fn in p.sched["sync"]:
                    fn(e)

            @block.gpsimd
            def _(e):
                for fn in p.sched["gpsimd"]:
                    fn(e)

            @block.tensor
            def _(e):
                for fn in p.sched["tensor"]:
                    fn(e)

            @block.vector
            def _(e):
                for fn in p.sched["vector"]:
                    fn(e)

            @block.scalar
            def _(e):
                for fn in p.sched["scalar"]:
                    fn(e)

    return nc, p


def _host_inputs(x, wq, wk, wv, wo, bq, bk, bv, W_fc, W_proj, core):
    xs = np.ascontiguousarray(x[core * B_PER:(core + 1) * B_PER].reshape(B_PER * L, E))
    d = {"x": xs}
    wqT = np.ascontiguousarray(wq.T)
    d["wq_fb"] = np.ascontiguousarray(
        wqT.reshape(16, 128, 16, 128).transpose(2, 1, 0, 3).reshape(HQ, 128, E))
    wkT = np.ascontiguousarray(wk.T)
    d["wk_fb"] = np.ascontiguousarray(
        wkT.reshape(16, 128, 2, 128).transpose(2, 1, 0, 3).reshape(HKV, 128, E))
    wvT = np.ascontiguousarray(wv.T)
    d["wv_rb"] = np.ascontiguousarray(wvT.reshape(16, 128, 256).transpose(1, 0, 2).reshape(128, 16 * 256))
    woT = np.ascontiguousarray(wo.T)
    wo_q = np.empty((4, 128, 16 * 512), np.float32)
    for oq in range(4):
        blk = woT[:, oq * 512:(oq + 1) * 512]
        wo_q[oq] = blk.reshape(16, 128, 512).transpose(1, 0, 2).reshape(128, 16 * 512)
    d["wo_qb"] = wo_q
    d["bq_t"] = np.ascontiguousarray(bq.reshape(16, 128).T)
    d["bk_t"] = np.ascontiguousarray(bk.reshape(2, 128).T)
    d["bv_row"] = np.ascontiguousarray(bv.reshape(1, 256))
    d["wfcT"] = np.ascontiguousarray(W_fc.T)
    d["wprT"] = np.ascontiguousarray(W_proj.T)
    ep = np.empty((128, 2), np.float32)
    ep[:, 0] = LN_EPS
    ep[:, 1] = RMS_EPS
    d["epsc"] = ep
    d["ident"] = np.eye(128, dtype=BF)
    d["onesrow"] = np.ones((1, 128), BF)
    d["onescol"] = np.ones((128, 1), BF)
    gm = np.zeros((128, 128), np.float32)
    gm[:64, :64] = 1.0 / 64
    gm[64:, 64:] = 1.0 / 64
    d["gmask"] = gm.astype(BF)
    ind = np.zeros((2, 128), np.float32)
    ind[0, :64] = 1.0
    ind[1, 64:] = 1.0
    d["indic2"] = ind.astype(BF)
    cm = np.zeros((128, 4 * 512), np.float32)
    kk = np.arange(128)[:, None]
    qq = np.arange(512)[None, :]
    for o in range(4):
        cm[:, o * 512:(o + 1) * 512] = np.where(kk + o * 128 <= qq, 0.0, NEG)
    d["cmask"] = cm.astype(BF)
    half = 64
    inv_freq = (10000.0 ** (-(np.arange(half, dtype=np.float32)) / np.float32(half))).astype(np.float32)
    pos = np.arange(L, dtype=np.float32)
    ang = (inv_freq[:, None] * pos[None, :]).astype(np.float32)
    cosv = np.cos(ang).astype(np.float32)
    sinv = np.sin(ang).astype(np.float32)
    d["ropec"] = np.concatenate([cosv, cosv], axis=0)
    d["ropes"] = np.concatenate([sinv, sinv], axis=0)
    return d


_NC_CACHE = {}


def kernel(L0_attn_x, L0_attn_wq, L0_attn_wk, L0_attn_wv, L0_attn_wo,
           L0_attn_bq, L0_attn_bk, L0_attn_bv, L0_ffn_W_fc, L0_ffn_W_proj):
    x = np.asarray(L0_attn_x, np.float32)
    args = [np.asarray(a, np.float32) for a in
            (L0_attn_wq, L0_attn_wk, L0_attn_wv, L0_attn_wo,
             L0_attn_bq, L0_attn_bk, L0_attn_bv, L0_ffn_W_fc, L0_ffn_W_proj)]
    if "nc" not in _NC_CACHE:
        _NC_CACHE["nc"] = build_program()
    nc, _ = _NC_CACHE["nc"]
    in_maps = [_host_inputs(x, *args, core=c) for c in range(NCORES)]
    res = run_bass_kernel_spmd(nc, in_maps, core_ids=list(range(NCORES)))
    out = np.empty((16, L, E), np.float32)
    for c in range(NCORES):
        out[c * B_PER:(c + 1) * B_PER] = np.asarray(res.results[c]["out"]).reshape(B_PER, L, E)
    return out


if __name__ == "__main__":
    import time
    t0 = time.time()
    nc, p = build_program()
    print("build ok: %.1fs, %d instructions" % (time.time() - t0, p.ninstr))
    print("sem counts:", p.counts)


# version 21 (final)
# speedup vs baseline: 1.0480x; 1.0480x over previous
import math
import numpy as np
import ml_dtypes
from contextlib import ExitStack

import concourse.bass as bass
import concourse.mybir as mybir
from concourse.bass_utils import run_bass_kernel_spmd

F32 = mybir.dt.float32
U16 = mybir.dt.uint16
BF16 = mybir.dt.bfloat16
AF = mybir.ActivationFunctionType
ALU = mybir.AluOpType
AX = mybir.AxisListType
BF = ml_dtypes.bfloat16

NCORES = 8
B_PER = 2
L = 2048
E = 2048
HQ, HKV, DH = 16, 2, 128
HID = 8192
SCALE = 1.0 / math.sqrt(DH)
LN_EPS = 1e-5
RMS_EPS = float(np.finfo(np.float32).eps)
NEG = -1e30

ENG_SEM = {"tensor": "PE", "vector": "DV", "scalar": "AC"}
DMA_ENG = ("sync", "gpsimd")


class Prog:
    def __init__(self, nc):
        self.nc = nc
        self.sched = {e: [] for e in ("sync", "gpsimd", "tensor", "vector", "scalar")}
        self.counts = {s: 0 for s in ("PE", "DV", "AC")}
        self.waited = {e: {} for e in self.sched}
        self.sems = {}
        self.alloc_sem = None
        self.ninstr = 0

    def _sem(self, name):
        if name not in self.sems:
            self.sems[name] = self.alloc_sem(name)
            self.counts.setdefault(name, 0)
        return self.sems[name]

    def wait(self, eng, sem, val):
        if val <= 0:
            return
        if val <= self.waited[eng].get(sem, 0):
            return
        assert val <= self.counts[sem], (eng, sem, val, self.counts[sem])
        self.waited[eng][sem] = val
        self.sched[eng].append(lambda e, s=sem, v=val: e.wait_ge(self.sems[s], v))
        self.ninstr += 1

    def run(self, eng, fn, inc=True, dsem=None):
        if eng in DMA_ENG and dsem is not None:
            sem = dsem
            self._sem(sem)
            n = 16
        elif eng == "gpsimd":
            sem = "PL"
            self._sem(sem)
            n = 1
        else:
            sem = ENG_SEM[eng]
            n = 1
        tok = None
        if inc:
            self.counts[sem] += n
            tok = (sem, self.counts[sem])

        def wrapped(e, fn=fn, tok=tok, sem=sem, n=n):
            ins = fn(e)
            if tok is not None:
                ins.then_inc(self.sems[sem], n)
        self.sched[eng].append(wrapped)
        self.ninstr += 1
        return tok

    def barrier(self):
        snap = dict(self.counts)
        for e in self.sched:
            for s, v in snap.items():
                self.wait(e, s, v)


class Rg:
    __slots__ = ("w", "r")

    def __init__(self):
        self.w = None
        self.r = []

    def read(self, prog, eng):
        if self.w:
            prog.wait(eng, *self.w)

    def noted_read(self, tok):
        if tok:
            self.r.append(tok)

    def write(self, prog, eng):
        best = {}
        if self.w:
            best[self.w[0]] = self.w[1]
        for s_, v_ in self.r:
            if v_ > best.get(s_, 0):
                best[s_] = v_
        for s_, v_ in best.items():
            prog.wait(eng, s_, v_)

    def noted_write(self, tok):
        self.w = tok
        self.r = []


def build_program(debug=False):
    nc = bass.Bass("TRN2", target_bir_lowering=False)
    p = Prog(nc)

    def din(name, shape, dt=F32):
        return nc.dram_tensor(name, shape, dt, kind="ExternalInput")

    x_h = din("x", [B_PER * L, E])
    wq_h = din("wq_fb", [HQ, 128, E])
    wk_h = din("wk_fb", [HKV, 128, E])
    wv_h = din("wv_rb", [128, 16 * 256])
    wo_h = din("wo_qb", [4, 128, 16 * 512])
    bq_h = din("bq_t", [128, 16])
    bk_h = din("bk_t", [128, 2])
    bv_h = din("bv_row", [1, 256])
    wfcT_h = din("wfcT", [E, HID])
    wprT_h = din("wprT", [HID, E])
    ident_h = din("ident", [128, 128], BF16)
    onesrow_h = din("onesrow", [1, 128], BF16)
    onescol_h = din("onescol", [128, 1], BF16)
    gmask_h = din("gmask", [128, 128], BF16)
    indic2_h = din("indic2", [2, 128], BF16)
    cmask_h = din("cmask", [128, 4 * 512], BF16)
    epsc_h = din("epsc", [128, 2])
    ropec_h = din("ropec", [128, L])
    ropes_h = din("ropes", [128, L])

    out_h = nc.dram_tensor("out", [B_PER * L, E], F32, kind="ExternalOutput")

    K_ = "ExternalOutput" if debug else "Internal"
    x2_h = nc.dram_tensor("x2scr", [B_PER * L, E], F32, kind=K_)
    wqs_h = nc.dram_tensor("wqscr", [HQ * 128, E], BF16, kind="Internal")
    wos_h = nc.dram_tensor("woscr", [4 * 128, 8192], BF16, kind="Internal")
    wb1_h = nc.dram_tensor("wb1scr", [E, HID], BF16, kind=K_)
    wb2_h = nc.dram_tensor("wb2scr", [HID, E], BF16, kind=K_)
    w1q_h = nc.dram_tensor("w1qscr", [16 * 128, 8192], BF16, kind=K_)
    w2q_h = nc.dram_tensor("w2qscr", [8 * 128, 16384], BF16, kind=K_)
    ssc_h = nc.dram_tensor("sscr", [512, 1024], BF16, kind=K_)
    sbb_h = nc.dram_tensor("sbbscr", [256 * 128, 1024], BF16, kind=K_)
    rhi_h = nc.dram_tensor("rhiscr", [512, 1024], BF16, kind=K_)
    rlo_h = nc.dram_tensor("rloscr", [512, 1024], BF16, kind=K_)

    es = ExitStack()
    with es:
        def sb(name, shape, dt=F32, stack=es):
            return stack.enter_context(nc.sbuf_tensor(name, shape, dt))

        ident = sb("identt", [128, 128], BF16)
        onesrow = sb("onesroww", [1, 128], BF16)
        onescol = sb("onescoll", [128, 1], BF16)
        gmask = sb("gmaskk", [128, 128], BF16)
        indic2 = sb("indic22", [2, 128], BF16)
        epsc = sb("epscc", [128, 2])

        PS = es.enter_context(nc.psum_tensor("PS", [128, 6 * 512], F32))
        PSB = es.enter_context(nc.psum_tensor("PSB", [128, 2 * 1024], BF16))
        ps_rg = [Rg() for _ in range(6)]
        psb_rg = [Rg(), Rg()]

        def bank(i):
            return PS[:, i * 512:(i + 1) * 512]

        p.alloc_sem = lambda name: es.enter_context(nc.semaphore(name))
        for s in ("PE", "DV", "AC"):
            p._sem(s)

        RG = {}

        def rg(name):
            if name not in RG:
                RG[name] = Rg()
            return RG[name]

        for dst, src in ((ident, ident_h), (onesrow, onesrow_h), (onescol, onescol_h),
                         (gmask, gmask_h), (indic2, indic2_h), (epsc, epsc_h)):
            rg(dst.name).write(p, "sync")
            t = p.run("sync", lambda e, d=dst, s=src: e.dma_start(out=d[:], in_=s.ap()), dsem="pre")
            rg(dst.name).noted_write(t)
        p.barrier()

        pc = ExitStack()
        pcb = [sb("pcb%d" % i, [128, 8192], BF16, pc) for i in range(2)]
        pcb_rg = [Rg(), Rg()]
        for i in range(20):
            bi = i % 2
            pcb_rg[bi].write(p, "gpsimd")
            if i < 16:
                t = p.run("gpsimd", lambda e, f=i, b=bi: e.dma_start(
                    out=pcb[b][:, 0:E], in_=bass.AP(wq_h, f * 128 * E, [[E, 128], [1, E]])),
                    dsem="pcl%d" % bi)
            else:
                t = p.run("gpsimd", lambda e, o=(i - 16), b=bi: e.dma_start(
                    out=pcb[b][:], in_=bass.AP(wo_h, o * 128 * 8192, [[8192, 128], [1, 8192]])),
                    dsem="pcl%d" % bi)
            pcb_rg[bi].noted_write(t)
            pcb_rg[bi].read(p, "sync")
            if i < 16:
                t = p.run("sync", lambda e, f=i, b=bi: e.dma_start(
                    out=bass.AP(wqs_h, f * 128 * E, [[E, 128], [1, E]]), in_=pcb[b][:, 0:E]),
                    dsem="pcw%d" % bi)
            else:
                t = p.run("sync", lambda e, o=(i - 16), b=bi: e.dma_start(
                    out=bass.AP(wos_h, o * 128 * 8192, [[8192, 128], [1, 8192]]), in_=pcb[b][:]),
                    dsem="pcw%d" % bi)
            pcb_rg[bi].noted_read(t)
        p.barrier()
        pc.close()

        qs_stack = ExitStack()
        QPAR = 3
        wbb = [sb("wbb%d" % i, [128, 1024], BF16, qs_stack) for i in range(QPAR)]
        ubq = [sb("ubq%d" % i, [128, 1024], BF16, qs_stack) for i in range(QPAR)]
        sbt = [sb("sbt%d" % i, [128, 1024], BF16, qs_stack) for i in range(QPAR)]
        wbb_rg = [Rg() for _ in range(QPAR)]
        ub_rg = [Rg() for _ in range(QPAR)]
        sbt_rg = [Rg() for _ in range(QPAR)]
        wbb2 = [sb("q2w%d" % i, [128, 1024], BF16, qs_stack) for i in range(QPAR)]
        sb2 = [sb("q2s%d" % i, [128, 1024], BF16, qs_stack) for i in range(QPAR)]
        stg_hi = [sb("stg_hi%d" % i, [2, 2 * 1024], BF16, qs_stack) for i in range(2)]
        stg_lo = [sb("stg_lo%d" % i, [2, 2 * 1024], BF16, qs_stack) for i in range(2)]
        tcb = [sb("q2t%d" % i, [128, 1024], BF16, qs_stack) for i in range(QPAR)]
        Abuf = [sb("q2A%d" % i, [128, 1024], BF16, qs_stack) for i in range(QPAR)]
        Bbuf = [sb("q2B%d" % i, [128, 1024], BF16, qs_stack) for i in range(QPAR)]
        qsb = [sb("q2q%d" % i, [128, 1024], BF16, qs_stack) for i in range(QPAR)]
        qpk = sb("qpk", [64, 512], F32, qs_stack)
        qrs = sb("qrs", [64, 512], F32, qs_stack)
        qhi = sb("qhi", [64, 512], BF16, qs_stack)
        qdl = sb("qdl", [64, 512], F32, qs_stack)
        qlo = sb("qlo", [64, 512], BF16, qs_stack)
        w2_rg = [Rg() for _ in range(QPAR)]
        s2_rg = [Rg() for _ in range(QPAR)]
        stg_rg = [[Rg(), Rg()], [Rg(), Rg()]]
        tc_rg = [Rg() for _ in range(QPAR)]
        A_rg = [Rg() for _ in range(QPAR)]
        B_rg = [Rg() for _ in range(QPAR)]
        qs_rg = [Rg() for _ in range(QPAR)]
        qp_rg = Rg()

        def q_tiles():
            ti = 0
            for dc in range(16):
                for hgr in range(8):
                    yield ti, 0, dc, hgr
                    ti += 1
            for hc in range(64):
                for dgr in range(2):
                    yield ti, 1, hc, dgr
                    ti += 1

        def emit_q1(ti, mat, r0, c0):
            bi = ti % QPAR
            src_h = wfcT_h if mat == 0 else wprT_h
            ncols = HID if mat == 0 else E
            wbb_rg[bi].write(p, "gpsimd")
            t = p.run("gpsimd", lambda e, s=src_h, r=r0, c=c0, w=ncols, b=bi:
                      e.dma_start(out=wbb[b][:],
                                  in_=bass.AP(s, r * 128 * w + c * 1024, [[w, 128], [1, 1024]])),
                      dsem="wbl%d" % bi)
            wbb_rg[bi].noted_write(t)
            wbb_rg[bi].read(p, "scalar")
            ub_rg[bi].write(p, "scalar")
            t = p.run("scalar", lambda e, b=bi: e.activation(ubq[b][:], wbb[b][:], AF.Abs))
            wbb_rg[bi].noted_read(t)
            ub_rg[bi].noted_write(t)
            ub_rg[bi].read(p, "tensor")
            rg("gmaskk").read(p, "tensor")
            ps_rg[2 * bi].write(p, "tensor")
            ps_rg[2 * bi + 1].write(p, "tensor")

            def q1mm(e, b=bi):
                e.matmul(bank(2 * b), gmask[:], ubq[b][:, 0:512], start=True, stop=True)
                return e.matmul(bank(2 * b + 1), gmask[:], ubq[b][:, 512:1024], start=True, stop=True)
            t = p.run("tensor", q1mm)
            ps_rg[2 * bi].noted_write(t)
            ps_rg[2 * bi + 1].noted_write(t)
            ub_rg[bi].noted_read(t)
            ps_rg[2 * bi].read(p, "vector")
            ps_rg[2 * bi + 1].read(p, "vector")
            sbt_rg[bi].write(p, "vector")
            t = p.run("vector", lambda e, b=bi: e.tensor_single_scalar(
                sbt[b][:], PS[:, 2 * b * 512: (2 * b + 2) * 512], 1e-8, ALU.max))
            ps_rg[2 * bi].noted_read(t)
            ps_rg[2 * bi + 1].noted_read(t)
            sbt_rg[bi].noted_write(t)
            sbt_rg[bi].write(p, "vector")
            t = p.run("vector", lambda e, b=bi: e.tensor_copy(
                sbt[b][:].bitcast(U16), sbt[b][:].bitcast(U16)))
            sbt_rg[bi].noted_write(t)
            sbt_rg[bi].read(p, "sync")
            t = p.run("sync", lambda e, ti_=ti, b=bi: e.dma_start(
                out=bass.AP(ssc_h, 2 * ti_ * 1024, [[1024, 2], [1, 1024]]),
                in_=bass.AP(sbt[b], 0, [[64 * 1024, 2], [1, 1024]])), dsem="ssw%d" % bi)
            sbt_rg[bi].noted_read(t)
            sbt_rg[bi].read(p, "sync")
            t = p.run("sync", lambda e, ti_=ti, b=bi: e.dma_start(
                out=bass.AP(sbb_h, ti_ * 128 * 1024, [[1024, 128], [1, 1024]]),
                in_=sbt[b][:]), dsem="sbw%d" % bi)
            sbt_rg[bi].noted_read(t)
            wbb_rg[bi].read(p, "sync")
            t = p.run("sync", lambda e, s=(wb1_h if mat == 0 else wb2_h), r=r0, c=c0, w=ncols, b=bi:
                      e.dma_start(out=bass.AP(s, r * 128 * w + c * 1024, [[w, 128], [1, 1024]]),
                                  in_=wbb[b][:]), dsem="wbw%d" % bi)
            wbb_rg[bi].noted_read(t)

        def emit_qp(ch):
            for i_ in range(QPAR):
                p.wait("gpsimd", "ssw%d" % i_, p.counts.get("ssw%d" % i_, 0))
            qp_rg.write(p, "gpsimd")
            t = p.run("gpsimd", lambda e, c=ch: e.dma_start(
                out=qpk[:], in_=bass.AP(ssc_h, c * 32 * 1024, [[512, 64], [1, 512]])), dsem="qpl")
            qp_rg.noted_write(t)
            p.wait("vector", *t)
            t0 = p.run("vector", lambda e: e.tensor_single_scalar(qpk[:], qpk[:], 1e-8, ALU.max))
            p.wait("vector", *t0)
            t1 = p.run("vector", lambda e: e.reciprocal(qrs[:], qpk[:]))
            p.wait("vector", *t1)
            t2 = p.run("vector", lambda e: e.tensor_copy(qhi[:], qrs[:]))
            p.wait("vector", *t2)
            t3 = p.run("vector", lambda e: e.tensor_tensor(qdl[:], qrs[:], qhi[:], ALU.subtract))
            p.wait("vector", *t3)
            tv = p.run("vector", lambda e: e.tensor_copy(qlo[:], qdl[:]))
            qp_rg.noted_write(tv)
            p.wait("sync", *tv)
            t = p.run("sync", lambda e, c=ch: e.dma_start(
                out=bass.AP(rhi_h, c * 32 * 1024, [[512, 64], [1, 512]]), in_=qhi[:]), dsem="qpw")
            qp_rg.noted_read(t)
            t = p.run("sync", lambda e, c=ch: e.dma_start(
                out=bass.AP(rlo_h, c * 32 * 1024, [[512, 64], [1, 512]]), in_=qlo[:]), dsem="qpw")
            qp_rg.noted_read(t)

        def emit_q2(ti, mat, r0, c0):
            if ti == 0:
                sl_ = ["qpw"] + ["sbw%d" % i_ for i_ in range(QPAR)] + ["wbw%d" % i_ for i_ in range(QPAR)]
                for engx in ("sync", "gpsimd"):
                    for s_ in sl_:
                        p.wait(engx, s_, p.counts.get(s_, 0))
            bi = ti % QPAR
            sp = (ti // 2) % 2
            if ti % 2 == 0:
                for si, (stg, srg, sc) in enumerate(((stg_hi, stg_rg[0], rhi_h), (stg_lo, stg_rg[1], rlo_h))):
                    srg[sp].write(p, "sync")
                    t = p.run("sync", lambda e, d=stg[sp], s=sc, ti_=ti: e.dma_start(
                        out=d[:], in_=bass.AP(s, 2 * ti_ * 1024, [[1024, 2], [2048, 2], [1, 1024]])),
                        dsem="stg%d_%d" % (si, sp))
                    srg[sp].noted_write(t)
            off = (ti % 2) * 1024
            src_h = wb1_h if mat == 0 else wb2_h
            ncols = HID if mat == 0 else E
            w2_rg[bi].write(p, "sync")
            t = p.run("sync", lambda e, s=src_h, r=r0, c=c0, w=ncols, b=bi: e.dma_start(
                out=wbb2[b][:], in_=bass.AP(s, r * 128 * w + c * 1024, [[w, 128], [1, 1024]])),
                dsem="w2l%d" % bi)
            w2_rg[bi].noted_write(t)
            s2_rg[bi].write(p, "sync")
            t = p.run("sync", lambda e, ti_=ti, b=bi: e.dma_start(
                out=sb2[b][:], in_=bass.AP(sbb_h, ti_ * 128 * 1024, [[1024, 128], [1, 1024]])),
                dsem="sbl%d" % bi)
            s2_rg[bi].noted_write(t)
            stg_rg[0][sp].read(p, "tensor")
            stg_rg[1][sp].read(p, "tensor")
            ps_rg[2 * bi].write(p, "tensor")
            ps_rg[2 * bi + 1].write(p, "tensor")

            def mmrs(e, o=off, b=bi, s_=sp):
                e.matmul(bank(2 * b), indic2[:], stg_hi[s_][:, o:o + 512], start=True, stop=False)
                e.matmul(bank(2 * b), indic2[:], stg_lo[s_][:, o:o + 512], start=False, stop=True)
                e.matmul(bank(2 * b + 1), indic2[:], stg_hi[s_][:, o + 512:o + 1024], start=True, stop=False)
                return e.matmul(bank(2 * b + 1), indic2[:], stg_lo[s_][:, o + 512:o + 1024], start=False, stop=True)
            t = p.run("tensor", mmrs)
            ps_rg[2 * bi].noted_write(t)
            ps_rg[2 * bi + 1].noted_write(t)
            stg_rg[0][sp].noted_read(t)
            stg_rg[1][sp].noted_read(t)
            w2_rg[bi].read(p, "vector")
            ps_rg[2 * bi].read(p, "vector")
            ps_rg[2 * bi + 1].read(p, "vector")
            tc_rg[bi].write(p, "vector")
            t = p.run("vector", lambda e, b=bi: e.tensor_tensor(
                tcb[b][:], wbb2[b][:], PS[:, 2 * b * 512: (2 * b + 2) * 512], ALU.mult))
            w2_rg[bi].noted_read(t)
            ps_rg[2 * bi].noted_read(t)
            ps_rg[2 * bi + 1].noted_read(t)
            tc_rg[bi].noted_write(t)
            tc_rg[bi].write(p, "vector")
            t = p.run("vector", lambda e, b=bi: e.tensor_copy(
                tcb[b][:].bitcast(U16), tcb[b][:].bitcast(U16)))
            tc_rg[bi].noted_write(t)
            tc_rg[bi].read(p, "vector")
            s2_rg[bi].read(p, "vector")
            A_rg[bi].write(p, "vector")
            t = p.run("vector", lambda e, b=bi: e.scalar_tensor_tensor(
                Abuf[b][:], tcb[b][:], 0.5, sb2[b][:], ALU.is_gt, ALU.mult))
            A_rg[bi].noted_write(t)
            B_rg[bi].write(p, "vector")
            t = p.run("vector", lambda e, b=bi: e.scalar_tensor_tensor(
                Bbuf[b][:], tcb[b][:], -0.5, sb2[b][:], ALU.is_lt, ALU.mult))
            tc_rg[bi].noted_read(t)
            s2_rg[bi].noted_read(t)
            B_rg[bi].noted_write(t)
            A_rg[bi].read(p, "vector")
            B_rg[bi].read(p, "vector")
            qs_rg[bi].write(p, "vector")
            t = p.run("vector", lambda e, b=bi: e.tensor_tensor(qsb[b][:], Abuf[b][:], Bbuf[b][:], ALU.subtract))
            A_rg[bi].noted_read(t)
            B_rg[bi].noted_read(t)
            qs_rg[bi].noted_write(t)
            qs_rg[bi].read(p, "gpsimd")
            if mat == 0:
                t = p.run("gpsimd", lambda e, o=(2 * c0) * 128 * 8192 + r0 * 512, b=bi:
                          e.dma_start(out=bass.AP(w1q_h, o, [[8192, 128], [128 * 8192, 2], [1, 512]]),
                                      in_=qsb[b][:]), dsem="qqw%d" % bi)
            else:
                t = p.run("gpsimd", lambda e, o=(4 * c0) * 128 * 16384 + r0 * 256, b=bi:
                          e.dma_start(out=bass.AP(w2q_h, o, [[16384, 128], [128 * 16384, 4], [1, 256]]),
                                      in_=qsb[b][:]), dsem="qqw%d" % bi)
            qs_rg[bi].noted_read(t)

        import functools
        quant_units = [functools.partial(emit_q1, *a) for a in q_tiles()]
        quant_units += [functools.partial(emit_qp, ch) for ch in range(16)]
        quant_units += [functools.partial(emit_q2, *a) for a in q_tiles()]
        quant_units.reverse()

        def drain_quant(k=1):
            for _ in range(k):
                if quant_units:
                    quant_units.pop()()

        while quant_units:
            drain_quant(8)
        p.barrier()
        qs_stack.close()

        ab = ExitStack()
        cmask = sb("cmaskk", [128, 4 * 512], BF16, ab)
        ropec = sb("ropecc", [128, L], F32, ab)
        ropes = sb("ropess", [128, L], F32, ab)
        bq_t = sb("bq_tt", [128, 16], F32, ab)
        bk_t = sb("bk_tt", [128, 2], F32, ab)
        bvrow = sb("bvroww", [1, 256], F32, ab)
        bvh = sb("bvhh", [1, 256], BF16, ab)
        bvl = sb("bvll", [1, 256], BF16, ab)
        vbias = sb("vbiass", [128, 256], F32, ab)
        wkb = sb("wkbb", [128, 2 * E], BF16, ab)
        wvb = sb("wvbb", [128, 16 * 256], BF16, ab)
        for dst_, src_ in ((cmask, cmask_h), (bq_t, bq_h), (bk_t, bk_h)):
            rg(dst_.name).write(p, "sync")
            t = p.run("sync", lambda e, d=dst_, s=src_: e.dma_start(out=d[:], in_=s.ap()), dsem="pre2")
            rg(dst_.name).noted_write(t)
        for dst_, src_ in ((wvb, wv_h),):
            rg(dst_.name).write(p, "gpsimd")
            t = p.run("gpsimd", lambda e, d=dst_, s=src_: e.dma_start(out=d[:], in_=s.ap()), dsem="preg2")
            rg(dst_.name).noted_write(t)
        for dst_, src_ in ((ropec, ropec_h), (ropes, ropes_h), (bvrow, bv_h)):
            rg(dst_.name).write(p, "sync")
            t = p.run("sync", lambda e, d=dst_, s=src_: e.dma_start(out=d[:], in_=s.ap()), dsem="pre2")
            rg(dst_.name).noted_write(t)
        rg("wkbb").write(p, "gpsimd")
        t = p.run("gpsimd", lambda e: e.dma_start(
            out=wkb[:], in_=bass.AP(wk_h, 0, [[E, 128], [128 * E, 2], [1, E]])), dsem="preg2")
        rg("wkbb").noted_write(t)
        p.barrier()
        rg("bvroww").read(p, "vector")
        rg("bvhh").write(p, "vector")
        t = p.run("vector", lambda e: e.tensor_copy(bvh[:], bvrow[:]))
        rg("bvhh").noted_write(t)
        rg("bvll").write(p, "vector")
        rg("bvhh").read(p, "vector")
        t = p.run("vector", lambda e: e.tensor_tensor(bvl[:], bvrow[:], bvh[:], ALU.subtract))
        rg("bvroww").noted_read(t)
        rg("bvhh").noted_read(t)
        rg("bvll").noted_write(t)
        rg("bvhh").read(p, "tensor")
        rg("bvll").read(p, "tensor")
        rg("onesroww").read(p, "tensor")
        ps_rg[0].write(p, "tensor")

        def bvmm(e):
            e.matmul(PS[:, 0:256], onesrow[:], bvh[:], start=True, stop=False)
            return e.matmul(PS[:, 0:256], onesrow[:], bvl[:], start=False, stop=True)
        t = p.run("tensor", bvmm)
        ps_rg[0].noted_write(t)
        ps_rg[0].read(p, "scalar")
        rg("vbiass").write(p, "scalar")
        t = p.run("scalar", lambda e: e.activation(vbias[:], PS[:, 0:256], AF.Copy))
        ps_rg[0].noted_read(t)
        rg("vbiass").noted_write(t)
        xa = sb("xa", [128, E], F32, ab)
        trash = [sb("trasha0", [128, E], BF16, ab), sb("trasha1", [128, E], BF16, ab)]
        stat = [sb("stata0", [128, 8], F32, ab), sb("stata1", [128, 8], F32, ab)]
        xln1t = sb("xln1t", [128, 16 * 512], BF16, ab)
        wqb = [sb("wqb0", [128, E], BF16, ab), sb("wqb1", [128, E], BF16, ab)]
        qevac = [sb("qevac0", [128, 512], BF16, ab), sb("qevac1", [128, 512], BF16, ab)]
        qswap = [sb("qswap0", [128, 512], BF16, ab), sb("qswap1", [128, 512], BF16, ab)]
        ropem = [sb("ropem0", [128, 512], F32, ab), sb("ropem1", [128, 512], F32, ab)]
        qtq = sb("qtq", [128, 16 * 512], BF16, ab)
        ktb = sb("ktb", [128, 2 * L], BF16, ab)
        vtb = sb("vtb", [128, 16 * 256], BF16, ab)
        ptb = [sb("ptb0", [128, 512], BF16, ab), sb("ptb1", [128, 512], BF16, ab)]
        srow = sb("srow", [1, 512], F32, ab)
        rhirow = sb("rhirow", [1, 512], BF16, ab)
        rlorow = sb("rlorow", [1, 512], BF16, ab)
        rbs = [sb("rbs0", [128, 512], F32, ab), sb("rbs1", [128, 512], F32, ab)]
        avt = sb("avt", [128, 16 * 512], BF16, ab)
        wob = [sb("wob0", [128, 16 * 512], BF16, ab)]
        xr = [sb("xr0", [128, 512], F32, ab), sb("xr1", [128, 512], F32, ab)]
        x2st = [sb("x2st0", [128, 512], F32, ab), sb("x2st1", [128, 512], F32, ab)]

        xa_rg = Rg(); xln_rg = Rg(); wq_rg = [Rg(), Rg()]; qe_rg = [Rg(), Rg()]
        qt_rg = Rg(); kt_rg = Rg(); vt_rg = Rg()
        pt_rg = [Rg(), Rg()]; row_rg = Rg(); rbs_rg = [Rg(), Rg()]; avt_rg = Rg()
        wo_rg = [Rg(), Rg()]; xr_rg = [Rg(), Rg()]; x2_rg = [Rg(), Rg()]
        stat_rg = [Rg(), Rg()]; trash_rg = [Rg(), Rg()]

        def ln_into(xsrc_h, row0, xabuf, xarg, trbuf, trrg, stbuf, strg, xsem):
            xarg.write(p, "sync")
            t = p.run("sync", lambda e: e.dma_start(
                out=xabuf[:], in_=bass.AP(xsrc_h, row0 * E, [[E, 128], [1, E]])), dsem=xsem)
            xarg.noted_write(t)
            xarg.read(p, "vector")
            strg.write(p, "vector")
            t = p.run("vector", lambda e: e.memset(stbuf[:, 1:2], 0.0))
            strg.noted_write(t)
            strg.write(p, "vector")
            xarg.read(p, "vector")
            t = p.run("vector", lambda e: e.reduce_sum(stbuf[:, 0:1], xabuf[:], axis=AX.X))
            xarg.noted_read(t)
            strg.noted_write(t)
            xarg.read(p, "scalar")
            strg.write(p, "scalar")
            trrg.write(p, "scalar")
            t = p.run("scalar", lambda e: e.activation(
                trbuf[:], xabuf[:], AF.Square, accum_out=stbuf[:, 1:2]))
            xarg.noted_read(t)
            trrg.noted_write(t)
            strg.noted_write(t)
            for fn in (
                lambda e: e.tensor_scalar(stbuf[:, 2:3], stbuf[:, 0:1], 1.0 / E, None, ALU.mult),
                lambda e: e.tensor_scalar(stbuf[:, 3:4], stbuf[:, 1:2], 1.0 / E, None, ALU.mult),
                lambda e: e.scalar_tensor_tensor(stbuf[:, 4:5], stbuf[:, 2:3], -1.0, stbuf[:, 2:3],
                                                 ALU.mult, ALU.mult),
                lambda e: e.tensor_tensor(stbuf[:, 4:5], stbuf[:, 3:4], stbuf[:, 4:5], ALU.add),
            ):
                strg.write(p, "vector")
                t = p.run("vector", fn)
                strg.noted_write(t)
            strg.write(p, "scalar")
            t = p.run("scalar", lambda e: e.activation(stbuf[:, 5:6], stbuf[:, 4:5], AF.Ln, bias=epsc[:, 0:1]))
            strg.noted_write(t)
            strg.write(p, "scalar")
            t = p.run("scalar", lambda e: e.activation(stbuf[:, 6:7], stbuf[:, 5:6], AF.Exp, scale=-0.5))
            strg.noted_write(t)
            strg.write(p, "vector")
            t = p.run("vector", lambda e: e.scalar_tensor_tensor(
                stbuf[:, 7:8], stbuf[:, 2:3], -1.0, stbuf[:, 6:7], ALU.mult, ALU.mult))
            strg.noted_write(t)
            xarg.read(p, "scalar")
            strg.read(p, "scalar")
            trrg.write(p, "scalar")
            t = p.run("scalar", lambda e: e.activation(
                trbuf[:], xabuf[:], AF.Identity, bias=stbuf[:, 7:8], scale=stbuf[:, 6:7]))
            xarg.noted_read(t)
            strg.noted_read(t)
            trrg.noted_write(t)

        def transpose_into(tsub, trbuf, trrg, dstbuf, dstrg):
            for grp in range(2):
                trrg.read(p, "tensor")
                psb_rg[grp].write(p, "tensor")

                def tp(e, g=grp):
                    for j in range(8):
                        c = g * 8 + j
                        ins = e.transpose(PSB[:, g * 1024 + j * 128: g * 1024 + (j + 1) * 128],
                                          trbuf[:, c * 128:(c + 1) * 128], ident[:])
                    return ins
                t = p.run("tensor", tp)
                psb_rg[grp].noted_write(t)
                trrg.noted_read(t)
                psb_rg[grp].read(p, "vector")
                dstrg.write(p, "vector")

                def ev(e, g=grp, ts=tsub):
                    return e.tensor_copy(
                        dstbuf[:].rearrange("p (dc t) -> p dc t", t=512)
                        [:, g * 8:(g + 1) * 8, ts * 128:(ts + 1) * 128],
                        PSB[:, g * 1024:(g + 1) * 1024].rearrange("p (j t) -> p j t", t=128))
                t = p.run("vector", ev)
                psb_rg[grp].noted_read(t)
                dstrg.noted_write(t)

        qsw_rg = [Rg(), Rg()]
        rm_rg = [Rg(), Rg()]

        def rope_into(qt_, dstbuf, dstrg, d0, par):
            qe = qevac[par]
            qs_ = qswap[par]
            qe_rg[par].read(p, "gpsimd")
            qsw_rg[par].write(p, "gpsimd")
            t = p.run("gpsimd", lambda e: e.dma_start(out=qs_[0:64, :], in_=qe[64:128, :]), dsem="swp%d" % par)
            qsw_rg[par].noted_write(t)
            t = p.run("gpsimd", lambda e: e.dma_start(out=qs_[64:128, :], in_=qe[0:64, :]), dsem="swp%d" % par)
            qsw_rg[par].noted_write(t)
            qe_rg[par].noted_read(t)
            qe_rg[par].read(p, "vector")
            qsw_rg[par].read(p, "vector")
            rg("ropecc").read(p, "vector")
            rg("ropess").read(p, "vector")
            dstrg.write(p, "vector")
            qt__, d0_ = qt_, d0
            csl = ropec[0:64, qt__ * 512:(qt__ + 1) * 512]
            snl = ropes[0:64, qt__ * 512:(qt__ + 1) * 512]
            csh = ropec[64:128, qt__ * 512:(qt__ + 1) * 512]
            snh = ropes[64:128, qt__ * 512:(qt__ + 1) * 512]
            rm_rg[0].write(p, "vector")
            t = p.run("vector", lambda e, a=csl, q_=qe: e.tensor_tensor(ropem[0][0:64, :], q_[0:64, :], a, ALU.mult))
            rm_rg[0].noted_write(t)
            rm_rg[1].write(p, "vector")
            t = p.run("vector", lambda e, a=snl, q_=qs_: e.tensor_tensor(ropem[1][0:64, :], q_[0:64, :], a, ALU.mult))
            rm_rg[1].noted_write(t)
            rm_rg[0].read(p, "vector")
            rm_rg[1].read(p, "vector")
            dstrg.write(p, "vector")
            t = p.run("vector", lambda e, d=d0_, db=dstbuf: e.tensor_tensor(
                db[0:64, d:d + 512], ropem[0][0:64, :], ropem[1][0:64, :], ALU.subtract))
            rm_rg[0].noted_read(t)
            rm_rg[1].noted_read(t)
            dstrg.noted_write(t)
            rm_rg[0].write(p, "vector")
            qe_rg[par].read(p, "vector")
            t = p.run("vector", lambda e, a=csh, q_=qe: e.tensor_tensor(ropem[0][64:128, :], q_[64:128, :], a, ALU.mult))
            rm_rg[0].noted_write(t)
            qe_rg[par].noted_read(t)
            rm_rg[1].write(p, "vector")
            qsw_rg[par].read(p, "vector")
            t = p.run("vector", lambda e, a=snh, q_=qs_: e.tensor_tensor(ropem[1][64:128, :], q_[64:128, :], a, ALU.mult))
            rm_rg[1].noted_write(t)
            qsw_rg[par].noted_read(t)
            rm_rg[0].read(p, "vector")
            rm_rg[1].read(p, "vector")
            dstrg.write(p, "vector")
            t = p.run("vector", lambda e, d=d0_, db=dstbuf: e.tensor_tensor(
                db[64:128, d:d + 512], ropem[0][64:128, :], ropem[1][64:128, :], ALU.add))
            rm_rg[0].noted_read(t)
            rm_rg[1].noted_read(t)
            dstrg.noted_write(t)

        for b in range(B_PER):
            for qt in range(4):
                nkc = 4 * (qt + 1)
                for tsub in range(4):
                    row0 = b * L + qt * 512 + tsub * 128
                    tp_ = tsub % 2
                    ln_into(x_h, row0, xa, xa_rg, trash[tp_], trash_rg[tp_], stat[tp_], stat_rg[tp_], "xal")
                    transpose_into(tsub, trash[tp_], trash_rg[tp_], xln1t, xln_rg)
                    drain_quant(2)
                for fc in range(HQ):
                    bi = fc % 2
                    wq_rg[bi].write(p, "sync")
                    t = p.run("sync", lambda e, f=fc, b_=bi: e.dma_start(
                        out=wqb[b_][:], in_=bass.AP(wqs_h, f * 128 * E, [[E, 128], [1, E]])),
                        dsem="wql%d" % bi)
                    wq_rg[bi].noted_write(t)
                    pb = fc % 2
                    wq_rg[bi].read(p, "tensor")
                    xln_rg.read(p, "tensor")
                    ps_rg[pb].write(p, "tensor")

                    def qmm(e, b_=bi, pb_=pb):
                        for dc in range(16):
                            ins = e.matmul(bank(pb_), wqb[b_][:, dc * 128:(dc + 1) * 128],
                                           xln1t[:, dc * 512:(dc + 1) * 512],
                                           start=(dc == 0), stop=(dc == 15))
                        return ins
                    t = p.run("tensor", qmm)
                    ps_rg[pb].noted_write(t)
                    wq_rg[bi].noted_read(t)
                    xln_rg.noted_read(t)
                    ps_rg[pb].read(p, "scalar")
                    rg("bq_tt").read(p, "scalar")
                    qe_rg[bi].write(p, "scalar")
                    t = p.run("scalar", lambda e, pb_=pb, f=fc, b_=bi: e.activation(
                        qevac[b_][:], bank(pb_), AF.Identity, bias=bq_t[:, f:f + 1]))
                    ps_rg[pb].noted_read(t)
                    qe_rg[bi].noted_write(t)
                    rope_into(qt, qtq, qt_rg, fc * 512, bi)
                    drain_quant(2)
                for kv in range(HKV):
                    pb = kv % 2
                    rg("wkbb").read(p, "tensor")
                    xln_rg.read(p, "tensor")
                    ps_rg[pb].write(p, "tensor")

                    def kmm(e, k=kv, pb_=pb):
                        for dc in range(16):
                            ins = e.matmul(bank(pb_), wkb[:, k * E + dc * 128: k * E + (dc + 1) * 128],
                                           xln1t[:, dc * 512:(dc + 1) * 512],
                                           start=(dc == 0), stop=(dc == 15))
                        return ins
                    t = p.run("tensor", kmm)
                    ps_rg[pb].noted_write(t)
                    xln_rg.noted_read(t)
                    ps_rg[pb].read(p, "scalar")
                    rg("bk_tt").read(p, "scalar")
                    qe_rg[kv].write(p, "scalar")
                    t = p.run("scalar", lambda e, pb_=pb, k=kv: e.activation(
                        qevac[k][:], bank(pb_), AF.Identity, bias=bk_t[:, k:k + 1]))
                    ps_rg[pb].noted_read(t)
                    qe_rg[kv].noted_write(t)
                    rope_into(qt, ktb, kt_rg, kv * L + qt * 512, kv)
                    drain_quant(1)
                for tsub in range(4):
                    pb = 2
                    rg("wvbb").read(p, "tensor")
                    xln_rg.read(p, "tensor")
                    ps_rg[pb].write(p, "tensor")

                    def vmm(e, ts=tsub, pb_=pb):
                        for dc in range(16):
                            ins = e.matmul(PS[:, pb_ * 512: pb_ * 512 + 256],
                                           xln1t[:, dc * 512 + ts * 128: dc * 512 + (ts + 1) * 128],
                                           wvb[:, dc * 256:(dc + 1) * 256],
                                           start=(dc == 0), stop=(dc == 15))
                        return ins
                    t = p.run("tensor", vmm)
                    ps_rg[pb].noted_write(t)
                    xln_rg.noted_read(t)
                    ps_rg[pb].read(p, "vector")
                    rg("vbiass").read(p, "vector")
                    vt_rg.write(p, "vector")
                    t = p.run("vector", lambda e, pb_=pb, ts=tsub, q_=qt: e.tensor_tensor(
                        vtb[:, (q_ * 4 + ts) * 256:(q_ * 4 + ts + 1) * 256],
                        PS[:, pb_ * 512: pb_ * 512 + 256], vbias[:], ALU.add))
                    ps_rg[pb].noted_read(t)
                    vt_rg.noted_write(t)
                    drain_quant(1)
                pending_tail = [None, None]
                for kv in range(HKV):
                    for g in range(8):
                        h = kv * 8 + g
                        hp = h % 2
                        avb = 2 if hp == 0 else 4
                        smb = 3 if hp == 0 else 5
                        def emit_S(kc):
                            o = kc - 4 * qt
                            sbk = kc % 2
                            kt_rg.read(p, "tensor")
                            qt_rg.read(p, "tensor")
                            ps_rg[sbk].write(p, "tensor")

                            def smm(e, k=kv, h_=h, kc_=kc, o_=o, sbk_=sbk):
                                ins = e.matmul(bank(sbk_),
                                               ktb[:, k * L + kc_ * 128: k * L + (kc_ + 1) * 128],
                                               qtq[:, h_ * 512:(h_ + 1) * 512],
                                               start=True, stop=(o_ < 0))
                                if o_ >= 0:
                                    ins = e.matmul(bank(sbk_), ident[:],
                                                   cmask[:, o_ * 512:(o_ + 1) * 512],
                                                   start=False, stop=True)
                                return ins
                            t = p.run("tensor", smm)
                            ps_rg[sbk].noted_write(t)
                            kt_rg.noted_read(t)
                            qt_rg.noted_read(t)

                        emit_S(0)
                        for kc in range(nkc):
                            sbk = kc % 2
                            ps_rg[sbk].read(p, "scalar")
                            pt_rg[sbk].write(p, "scalar")
                            t = p.run("scalar", lambda e, sbk_=sbk: e.activation(
                                ptb[sbk_][:], bank(sbk_), AF.Exp, scale=SCALE))
                            ps_rg[sbk].noted_read(t)
                            pt_rg[sbk].noted_write(t)
                            if kc + 1 < nkc:
                                emit_S(kc + 1)
                            if kc == 0 and pending_tail[0] is not None:
                                pending_tail[0]()
                                pending_tail[0] = None
                            if kc == 2 and pending_tail[1] is not None:
                                pending_tail[1]()
                                pending_tail[1] = None
                            pt_rg[sbk].read(p, "tensor")
                            vt_rg.read(p, "tensor")
                            ps_rg[avb].write(p, "tensor")
                            ps_rg[smb].write(p, "tensor")

                            def avmm(e, k=kv, kc_=kc, sbk_=sbk, first=(kc == 0), last=(kc == nkc - 1),
                                     avb_=avb, smb_=smb):
                                e.matmul(bank(avb_),
                                         vtb[:, kc_ * 256 + k * 128: kc_ * 256 + k * 128 + 128],
                                         ptb[sbk_][:], start=first, stop=last)
                                return e.matmul(PS[0:1, smb_ * 512:(smb_ + 1) * 512], onescol[:], ptb[sbk_][:],
                                                start=first, stop=last)
                            t = p.run("tensor", avmm)
                            ps_rg[avb].noted_write(t)
                            ps_rg[smb].noted_write(t)
                            pt_rg[sbk].noted_read(t)
                            vt_rg.noted_read(t)
                        def tail1(h=h, hp=hp, avb=avb, smb=smb):
                            ps_rg[smb].read(p, "vector")
                            row_rg.write(p, "vector")
                            t = p.run("vector", lambda e, smb_=smb: e.tensor_copy(
                                srow[:], PS[0:1, smb_ * 512:(smb_ + 1) * 512]))
                            ps_rg[smb].noted_read(t)
                            row_rg.noted_write(t)
                            for fn in (lambda e: e.activation(srow[:], srow[:], AF.Ln),
                                       lambda e: e.activation(srow[:], srow[:], AF.Exp, scale=-1.0)):
                                row_rg.write(p, "scalar")
                                t = p.run("scalar", fn)
                                row_rg.noted_write(t)
                            row_rg.write(p, "vector")
                            t = p.run("vector", lambda e: e.tensor_copy(rhirow[:], srow[:]))
                            row_rg.noted_write(t)
                            row_rg.write(p, "vector")
                            t = p.run("vector", lambda e: e.tensor_tensor(
                                rlorow[:], srow[:], rhirow[:], ALU.subtract))
                            row_rg.noted_write(t)

                        def tail2(h=h, hp=hp, avb=avb, smb=smb):
                            row_rg.read(p, "tensor")
                            rg("onesroww").read(p, "tensor")
                            ps_rg[smb].write(p, "tensor")

                            def rbmm(e, smb_=smb):
                                e.matmul(bank(smb_), onesrow[:], rhirow[:], start=True, stop=False)
                                return e.matmul(bank(smb_), onesrow[:], rlorow[:], start=False, stop=True)
                            t = p.run("tensor", rbmm)
                            ps_rg[smb].noted_write(t)
                            row_rg.noted_read(t)
                            ps_rg[smb].read(p, "scalar")
                            rbs_rg[hp].write(p, "scalar")
                            t = p.run("scalar", lambda e, smb_=smb, hp_=hp: e.activation(
                                rbs[hp_][:], bank(smb_), AF.Copy))
                            ps_rg[smb].noted_read(t)
                            rbs_rg[hp].noted_write(t)
                            ps_rg[avb].read(p, "vector")
                            rbs_rg[hp].read(p, "vector")
                            avt_rg.write(p, "vector")
                            t = p.run("vector", lambda e, h_=h, avb_=avb, hp_=hp: e.tensor_tensor(
                                avt[:, h_ * 512:(h_ + 1) * 512], bank(avb_), rbs[hp_][:], ALU.mult))
                            ps_rg[avb].noted_read(t)
                            rbs_rg[hp].noted_read(t)
                            avt_rg.noted_write(t)
                            drain_quant(2)
                        pending_tail[0] = tail1
                        pending_tail[1] = tail2
                for pi in range(2):
                    if pending_tail[pi] is not None:
                        pending_tail[pi]()
                        pending_tail[pi] = None
                for oq in range(4):
                    wp = 0
                    wo_rg[wp].write(p, "sync")
                    t = p.run("sync", lambda e, o=oq, w_=wp: e.dma_start(
                        out=wob[w_][:], in_=bass.AP(wos_h, o * 128 * 8192,
                                                    [[8192, 128], [1, 8192]])), dsem="wol%d" % wp)
                    wo_rg[wp].noted_write(t)
                    for tsub in range(4):
                        sl = tsub % 2
                        r0 = b * L + qt * 512 + tsub * 128
                        xr_rg[sl].write(p, "sync")
                        t = p.run("sync", lambda e, r_=r0, o=oq, sl_=sl: e.dma_start(
                            out=xr[sl_][:], in_=bass.AP(x_h, r_ * E + o * 512, [[E, 128], [1, 512]])),
                            dsem="xrl%d" % sl)
                        xr_rg[sl].noted_write(t)
                        pb = tsub % 2
                        avt_rg.read(p, "tensor")
                        wo_rg[wp].read(p, "tensor")
                        ps_rg[pb].write(p, "tensor")

                        def omm(e, ts=tsub, pb_=pb, w_=wp):
                            for dc in range(16):
                                ins = e.matmul(bank(pb_),
                                               avt[:, dc * 512 + ts * 128: dc * 512 + (ts + 1) * 128],
                                               wob[w_][:, dc * 512:(dc + 1) * 512],
                                               start=(dc == 0), stop=(dc == 15))
                            return ins
                        t = p.run("tensor", omm)
                        ps_rg[pb].noted_write(t)
                        avt_rg.noted_read(t)
                        wo_rg[wp].noted_read(t)
                        ps_rg[pb].read(p, "vector")
                        xr_rg[sl].read(p, "vector")
                        x2_rg[sl].write(p, "vector")
                        t = p.run("vector", lambda e, pb_=pb, sl_=sl: e.tensor_tensor(
                            x2st[sl_][:], bank(pb_), xr[sl_][:], ALU.add))
                        ps_rg[pb].noted_read(t)
                        xr_rg[sl].noted_read(t)
                        x2_rg[sl].noted_write(t)
                        x2_rg[sl].read(p, "gpsimd")
                        t = p.run("gpsimd", lambda e, r_=r0, o=oq, sl_=sl: e.dma_start(
                            out=bass.AP(x2_h, r_ * E + o * 512, [[E, 128], [1, 512]]),
                            in_=x2st[sl_][:]), dsem="x2w%d" % sl)
                        x2_rg[sl].noted_read(t)
                    drain_quant(1)
        p.barrier()
        ab.close()

        cc = ExitStack()
        xc = sb("xc", [128, E], F32, cc)
        trc = [sb("trc0", [128, E], BF16, cc), sb("trc1", [128, E], BF16, cc)]
        stc = [sb("stc0", [128, 8], F32, cc), sb("stc1", [128, 8], F32, cc)]
        xln2t = sb("xln2t", [128, 16 * 512], BF16, cc)
        w1b = [sb("w1b0", [128, 16 * 512], BF16, cc), sb("w1b1", [128, 16 * 512], BF16, cc)]
        relf = sb("relf", [128, 512], BF16, cc)
        h2t = sb("h2t", [128, 64 * 512], BF16, cc)
        dtmp = sb("dtmp", [128, 128], F32, cc)
        invb = sb("invb", [128, 8], F32, cc)
        w2b = [sb("w2b0", [128, 64 * 256], BF16, cc), sb("w2b1", [128, 64 * 256], BF16, cc)]
        x2r = [sb("x2r0", [128, 256], F32, cc), sb("x2r1", [128, 256], F32, cc)]
        outst = [sb("outst0", [128, 256], F32, cc), sb("outst1", [128, 256], F32, cc)]

        xc_rg = Rg(); trc_rg = [Rg(), Rg()]; stc_rg = [Rg(), Rg()]; xl2_rg = Rg(); dt_rg = Rg()
        w1_rg = [Rg(), Rg()]; rel_rg = Rg(); h2_rg = Rg()
        inv_rg = Rg(); w2q_rg = [Rg(), Rg()]; x2r_rg = [Rg(), Rg()]; out_rg = [Rg(), Rg()]

        for tile in range(8):
            base = tile * 512
            for tsub in range(4):
                tp_ = tsub % 2
                ln_into(x2_h, base + tsub * 128, xc, xc_rg, trc[tp_], trc_rg[tp_], stc[tp_], stc_rg[tp_], "xcl")
                transpose_into(tsub, trc[tp_], trc_rg[tp_], xln2t, xl2_rg)
            for hg in range(16):
                bi = hg % 2
                w1_rg[bi].write(p, "gpsimd")
                t = p.run("gpsimd", lambda e, h=hg, b_=bi: e.dma_start(
                    out=w1b[b_][:],
                    in_=bass.AP(w1q_h, h * 128 * 8192, [[8192, 128], [1, 8192]])), dsem="w1l%d" % bi)
                w1_rg[bi].noted_write(t)
                for hc4 in range(4):
                    hc = hg * 4 + hc4
                    pb = hc % 2
                    w1_rg[bi].read(p, "tensor")
                    xl2_rg.read(p, "tensor")
                    ps_rg[pb].write(p, "tensor")

                    def f1mm(e, b_=bi, h4=hc4, pb_=pb):
                        for dc in range(16):
                            ins = e.matmul(bank(pb_),
                                           w1b[b_][:, dc * 512 + h4 * 128: dc * 512 + (h4 + 1) * 128],
                                           xln2t[:, dc * 512:(dc + 1) * 512],
                                           start=(dc == 0), stop=(dc == 15))
                        return ins
                    t = p.run("tensor", f1mm)
                    ps_rg[pb].noted_write(t)
                    w1_rg[bi].noted_read(t)
                    xl2_rg.noted_read(t)
                    ps_rg[pb].read(p, "scalar")
                    rel_rg.write(p, "scalar")
                    t = p.run("scalar", lambda e, pb_=pb: e.activation(relf[:], bank(pb_), AF.Relu))
                    ps_rg[pb].noted_read(t)
                    rel_rg.noted_write(t)
                    rel_rg.read(p, "scalar")
                    h2_rg.write(p, "scalar")
                    t = p.run("scalar", lambda e, h_=hc: e.activation(
                        h2t[:, h_ * 512:(h_ + 1) * 512], relf[:], AF.Square))
                    rel_rg.noted_read(t)
                    h2_rg.noted_write(t)
                h2_rg.read(p, "tensor")
                ps_rg[2].write(p, "tensor")

                def dmm(e, hg_=hg):
                    for hc4 in range(4):
                        h_ = hg_ * 4 + hc4
                        for ts in range(4):
                            ins = e.matmul(PS[:, 2 * 512 + ts * 128: 2 * 512 + (ts + 1) * 128],
                                           h2t[:, h_ * 512 + ts * 128: h_ * 512 + (ts + 1) * 128],
                                           h2t[:, h_ * 512 + ts * 128: h_ * 512 + (ts + 1) * 128],
                                           start=(h_ == 0 and ts == 0), stop=(h_ == 63 and ts == 3))
                    return ins
                t = p.run("tensor", dmm)
                ps_rg[2].noted_write(t)
                h2_rg.noted_read(t)
            for tsub in range(4):
                ps_rg[2].read(p, "vector")
                dt_rg.write(p, "vector")
                t = p.run("vector", lambda e, ts=tsub: e.tensor_tensor(
                    dtmp[:], PS[:, 2 * 512 + ts * 128: 2 * 512 + (ts + 1) * 128], ident[:], ALU.mult))
                ps_rg[2].noted_read(t)
                dt_rg.noted_write(t)
                dt_rg.read(p, "vector")
                inv_rg.write(p, "vector")
                t = p.run("vector", lambda e, ts=tsub: e.reduce_sum(invb[:, ts:ts + 1], dtmp[:], axis=AX.X))
                dt_rg.noted_read(t)
                inv_rg.noted_write(t)
                inv_rg.write(p, "vector")
                t = p.run("vector", lambda e, ts=tsub: e.tensor_scalar(
                    invb[:, 4 + ts:5 + ts], invb[:, ts:ts + 1], 1.0 / HID, None, ALU.mult))
                inv_rg.noted_write(t)
                inv_rg.write(p, "scalar")
                t = p.run("scalar", lambda e, ts=tsub: e.activation(
                    invb[:, ts:ts + 1], invb[:, 4 + ts:5 + ts], AF.Ln, bias=epsc[:, 1:2]))
                inv_rg.noted_write(t)
                inv_rg.write(p, "scalar")
                t = p.run("scalar", lambda e, ts=tsub: e.activation(
                    invb[:, 4 + ts:5 + ts], invb[:, ts:ts + 1], AF.Exp, scale=-0.5))
                inv_rg.noted_write(t)
            for dq in range(8):
                wb_ = dq % 2
                w2q_rg[wb_].write(p, "gpsimd")
                t = p.run("gpsimd", lambda e, d=dq, b_=wb_: e.dma_start(
                    out=w2b[b_][:],
                    in_=bass.AP(w2q_h, d * 128 * 16384, [[16384, 128], [1, 16384]])),
                    dsem="w2s%d" % wb_)
                w2q_rg[wb_].noted_write(t)
                for tsub in range(4):
                    pb = 3 + (tsub % 2)
                    sl = tsub % 2
                    h2_rg.read(p, "tensor")
                    w2q_rg[wb_].read(p, "tensor")
                    ps_rg[pb].write(p, "tensor")

                    def f2mm(e, ts=tsub, pb_=pb, b_=wb_):
                        for hc in range(64):
                            ins = e.matmul(PS[:, pb_ * 512: pb_ * 512 + 256],
                                           h2t[:, hc * 512 + ts * 128: hc * 512 + (ts + 1) * 128],
                                           w2b[b_][:, hc * 256:(hc + 1) * 256],
                                           start=(hc == 0), stop=(hc == 63))
                        return ins
                    t = p.run("tensor", f2mm)
                    ps_rg[pb].noted_write(t)
                    h2_rg.noted_read(t)
                    w2q_rg[wb_].noted_read(t)
                    x2r_rg[sl].write(p, "sync")
                    t = p.run("sync", lambda e, ts=tsub, d=dq, b_=base, sl_=sl: e.dma_start(
                        out=x2r[sl_][:],
                        in_=bass.AP(x2_h, (b_ + ts * 128) * E + d * 256, [[E, 128], [1, 256]])),
                        dsem="x2r%d" % sl)
                    x2r_rg[sl].noted_write(t)
                    ps_rg[pb].read(p, "vector")
                    x2r_rg[sl].read(p, "vector")
                    inv_rg.read(p, "vector")
                    out_rg[sl].write(p, "vector")
                    t = p.run("vector", lambda e, ts=tsub, pb_=pb, sl_=sl: e.scalar_tensor_tensor(
                        outst[sl_][:], PS[:, pb_ * 512: pb_ * 512 + 256],
                        invb[:, 4 + ts:5 + ts], x2r[sl_][:], ALU.mult, ALU.add))
                    ps_rg[pb].noted_read(t)
                    x2r_rg[sl].noted_read(t)
                    inv_rg.noted_read(t)
                    out_rg[sl].noted_write(t)
                    out_rg[sl].read(p, "sync")
                    t = p.run("sync", lambda e, ts=tsub, d=dq, b_=base, sl_=sl: e.dma_start(
                        out=bass.AP(out_h, (b_ + ts * 128) * E + d * 256, [[E, 128], [1, 256]]),
                        in_=outst[sl_][:]), dsem="osw%d" % sl)
                    out_rg[sl].noted_read(t)
        cc.close()

        p.barrier()

        with nc.Block() as block:
            @block.sync
            def _(e):
                for fn in p.sched["sync"]:
                    fn(e)

            @block.gpsimd
            def _(e):
                for fn in p.sched["gpsimd"]:
                    fn(e)

            @block.tensor
            def _(e):
                for fn in p.sched["tensor"]:
                    fn(e)

            @block.vector
            def _(e):
                for fn in p.sched["vector"]:
                    fn(e)

            @block.scalar
            def _(e):
                for fn in p.sched["scalar"]:
                    fn(e)

    return nc, p


def _host_inputs(x, wq, wk, wv, wo, bq, bk, bv, W_fc, W_proj, core):
    xs = np.ascontiguousarray(x[core * B_PER:(core + 1) * B_PER].reshape(B_PER * L, E))
    d = {"x": xs}
    wqT = np.ascontiguousarray(wq.T)
    d["wq_fb"] = np.ascontiguousarray(
        wqT.reshape(16, 128, 16, 128).transpose(2, 1, 0, 3).reshape(HQ, 128, E))
    wkT = np.ascontiguousarray(wk.T)
    d["wk_fb"] = np.ascontiguousarray(
        wkT.reshape(16, 128, 2, 128).transpose(2, 1, 0, 3).reshape(HKV, 128, E))
    wvT = np.ascontiguousarray(wv.T)
    d["wv_rb"] = np.ascontiguousarray(wvT.reshape(16, 128, 256).transpose(1, 0, 2).reshape(128, 16 * 256))
    woT = np.ascontiguousarray(wo.T)
    wo_q = np.empty((4, 128, 16 * 512), np.float32)
    for oq in range(4):
        blk = woT[:, oq * 512:(oq + 1) * 512]
        wo_q[oq] = blk.reshape(16, 128, 512).transpose(1, 0, 2).reshape(128, 16 * 512)
    d["wo_qb"] = wo_q
    d["bq_t"] = np.ascontiguousarray(bq.reshape(16, 128).T)
    d["bk_t"] = np.ascontiguousarray(bk.reshape(2, 128).T)
    d["bv_row"] = np.ascontiguousarray(bv.reshape(1, 256))
    d["wfcT"] = np.ascontiguousarray(W_fc.T)
    d["wprT"] = np.ascontiguousarray(W_proj.T)
    ep = np.empty((128, 2), np.float32)
    ep[:, 0] = LN_EPS
    ep[:, 1] = RMS_EPS
    d["epsc"] = ep
    d["ident"] = np.eye(128, dtype=BF)
    d["onesrow"] = np.ones((1, 128), BF)
    d["onescol"] = np.ones((128, 1), BF)
    gm = np.zeros((128, 128), np.float32)
    gm[:64, :64] = 1.0 / 64
    gm[64:, 64:] = 1.0 / 64
    d["gmask"] = gm.astype(BF)
    ind = np.zeros((2, 128), np.float32)
    ind[0, :64] = 1.0
    ind[1, 64:] = 1.0
    d["indic2"] = ind.astype(BF)
    cm = np.zeros((128, 4 * 512), np.float32)
    kk = np.arange(128)[:, None]
    qq = np.arange(512)[None, :]
    for o in range(4):
        cm[:, o * 512:(o + 1) * 512] = np.where(kk + o * 128 <= qq, 0.0, NEG)
    d["cmask"] = cm.astype(BF)
    half = 64
    inv_freq = (10000.0 ** (-(np.arange(half, dtype=np.float32)) / np.float32(half))).astype(np.float32)
    pos = np.arange(L, dtype=np.float32)
    ang = (inv_freq[:, None] * pos[None, :]).astype(np.float32)
    cosv = np.cos(ang).astype(np.float32)
    sinv = np.sin(ang).astype(np.float32)
    d["ropec"] = np.concatenate([cosv, cosv], axis=0)
    d["ropes"] = np.concatenate([sinv, sinv], axis=0)
    return d


_NC_CACHE = {}


def kernel(L0_attn_x, L0_attn_wq, L0_attn_wk, L0_attn_wv, L0_attn_wo,
           L0_attn_bq, L0_attn_bk, L0_attn_bv, L0_ffn_W_fc, L0_ffn_W_proj):
    x = np.asarray(L0_attn_x, np.float32)
    args = [np.asarray(a, np.float32) for a in
            (L0_attn_wq, L0_attn_wk, L0_attn_wv, L0_attn_wo,
             L0_attn_bq, L0_attn_bk, L0_attn_bv, L0_ffn_W_fc, L0_ffn_W_proj)]
    if "nc" not in _NC_CACHE:
        _NC_CACHE["nc"] = build_program()
    nc, _ = _NC_CACHE["nc"]
    in_maps = [_host_inputs(x, *args, core=c) for c in range(NCORES)]
    res = run_bass_kernel_spmd(nc, in_maps, core_ids=list(range(NCORES)))
    out = np.empty((16, L, E), np.float32)
    for c in range(NCORES):
        out[c * B_PER:(c + 1) * B_PER] = np.asarray(res.results[c]["out"]).reshape(B_PER, L, E)
    return out


if __name__ == "__main__":
    import time
    t0 = time.time()
    nc, p = build_program()
    print("build ok: %.1fs, %d instructions" % (time.time() - t0, p.ninstr))
    print("sem counts:", p.counts)


# version 22
# speedup vs baseline: 1.0707x; 1.0216x over previous
import math
import numpy as np
import ml_dtypes
from contextlib import ExitStack

import concourse.bass as bass
import concourse.mybir as mybir
from concourse.bass_utils import run_bass_kernel_spmd

F32 = mybir.dt.float32
U16 = mybir.dt.uint16
BF16 = mybir.dt.bfloat16
AF = mybir.ActivationFunctionType
ALU = mybir.AluOpType
AX = mybir.AxisListType
BF = ml_dtypes.bfloat16

NCORES = 8
B_PER = 2
L = 2048
E = 2048
HQ, HKV, DH = 16, 2, 128
HID = 8192
SCALE = 1.0 / math.sqrt(DH)
LN_EPS = 1e-5
RMS_EPS = float(np.finfo(np.float32).eps)
NEG = -1e30

ENG_SEM = {"tensor": "PE", "vector": "DV", "scalar": "AC"}
DMA_ENG = ("sync", "gpsimd")


class Prog:
    def __init__(self, nc):
        self.nc = nc
        self.sched = {e: [] for e in ("sync", "gpsimd", "tensor", "vector", "scalar")}
        self.counts = {s: 0 for s in ("PE", "DV", "AC")}
        self.waited = {e: {} for e in self.sched}
        self.sems = {}
        self.alloc_sem = None
        self.ninstr = 0

    def _sem(self, name):
        if name not in self.sems:
            self.sems[name] = self.alloc_sem(name)
            self.counts.setdefault(name, 0)
        return self.sems[name]

    def wait(self, eng, sem, val):
        if val <= 0:
            return
        if val <= self.waited[eng].get(sem, 0):
            return
        assert val <= self.counts[sem], (eng, sem, val, self.counts[sem])
        self.waited[eng][sem] = val
        self.sched[eng].append(lambda e, s=sem, v=val: e.wait_ge(self.sems[s], v))
        self.ninstr += 1

    def run(self, eng, fn, inc=True, dsem=None):
        if eng in DMA_ENG and dsem is not None:
            sem = dsem
            self._sem(sem)
            n = 16
        elif eng == "gpsimd":
            sem = "PL"
            self._sem(sem)
            n = 1
        else:
            sem = ENG_SEM[eng]
            n = 1
        tok = None
        if inc:
            self.counts[sem] += n
            tok = (sem, self.counts[sem])

        def wrapped(e, fn=fn, tok=tok, sem=sem, n=n):
            ins = fn(e)
            if tok is not None:
                ins.then_inc(self.sems[sem], n)
        self.sched[eng].append(wrapped)
        self.ninstr += 1
        return tok

    def barrier(self):
        snap = dict(self.counts)
        for e in self.sched:
            for s, v in snap.items():
                self.wait(e, s, v)


class Rg:
    __slots__ = ("w", "r")

    def __init__(self):
        self.w = None
        self.r = []

    def read(self, prog, eng):
        if self.w:
            prog.wait(eng, *self.w)

    def noted_read(self, tok):
        if tok:
            self.r.append(tok)

    def write(self, prog, eng):
        best = {}
        if self.w:
            best[self.w[0]] = self.w[1]
        for s_, v_ in self.r:
            if v_ > best.get(s_, 0):
                best[s_] = v_
        for s_, v_ in best.items():
            prog.wait(eng, s_, v_)

    def noted_write(self, tok):
        self.w = tok
        self.r = []


def build_program(debug=False):
    nc = bass.Bass("TRN2", target_bir_lowering=False)
    p = Prog(nc)

    def din(name, shape, dt=F32):
        return nc.dram_tensor(name, shape, dt, kind="ExternalInput")

    x_h = din("x", [B_PER * L, E])
    wq_h = din("wq_fb", [HQ, 128, E])
    wk_h = din("wk_fb", [HKV, 128, E])
    wv_h = din("wv_rb", [128, 16 * 256])
    wo_h = din("wo_qb", [4, 128, 16 * 512])
    bq_h = din("bq_t", [128, 16])
    bk_h = din("bk_t", [128, 2])
    bv_h = din("bv_row", [1, 256])
    wfcT_h = din("wfcT", [E, HID])
    wprT_h = din("wprT", [HID, E])
    ident_h = din("ident", [128, 128], BF16)
    onesrow_h = din("onesrow", [1, 128], BF16)
    onescol_h = din("onescol", [128, 1], BF16)
    gmask_h = din("gmask", [128, 128], BF16)
    indic2_h = din("indic2", [2, 128], BF16)
    cmask_h = din("cmask", [128, 4 * 512], BF16)
    epsc_h = din("epsc", [128, 2])
    ropec_h = din("ropec", [128, L])
    ropes_h = din("ropes", [128, L])

    out_h = nc.dram_tensor("out", [B_PER * L, E], F32, kind="ExternalOutput")

    K_ = "ExternalOutput" if debug else "Internal"
    x2_h = nc.dram_tensor("x2scr", [B_PER * L, E], F32, kind=K_)
    wqs_h = nc.dram_tensor("wqscr", [HQ * 128, E], BF16, kind="Internal")
    wos_h = nc.dram_tensor("woscr", [4 * 128, 8192], BF16, kind="Internal")
    wb1_h = nc.dram_tensor("wb1scr", [E, HID], BF16, kind=K_)
    wb2_h = nc.dram_tensor("wb2scr", [HID, E], BF16, kind=K_)
    w1q_h = nc.dram_tensor("w1qscr", [16 * 128, 8192], BF16, kind=K_)
    w2q_h = nc.dram_tensor("w2qscr", [8 * 128, 16384], BF16, kind=K_)
    ssc_h = nc.dram_tensor("sscr", [512, 1024], BF16, kind=K_)
    sbb_h = nc.dram_tensor("sbbscr", [256 * 128, 1024], BF16, kind=K_)
    rhi_h = nc.dram_tensor("rhiscr", [512, 1024], BF16, kind=K_)
    rlo_h = nc.dram_tensor("rloscr", [512, 1024], BF16, kind=K_)

    es = ExitStack()
    with es:
        def sb(name, shape, dt=F32, stack=es):
            return stack.enter_context(nc.sbuf_tensor(name, shape, dt))

        ident = sb("identt", [128, 128], BF16)
        onesrow = sb("onesroww", [1, 128], BF16)
        onescol = sb("onescoll", [128, 1], BF16)
        gmask = sb("gmaskk", [128, 128], BF16)
        indic2 = sb("indic22", [2, 128], BF16)
        epsc = sb("epscc", [128, 2])

        PS = es.enter_context(nc.psum_tensor("PS", [128, 6 * 512], F32))
        PSB = es.enter_context(nc.psum_tensor("PSB", [128, 2 * 1024], BF16))
        ps_rg = [Rg() for _ in range(6)]
        psb_rg = [Rg(), Rg()]

        def bank(i):
            return PS[:, i * 512:(i + 1) * 512]

        p.alloc_sem = lambda name: es.enter_context(nc.semaphore(name))
        for s in ("PE", "DV", "AC"):
            p._sem(s)

        RG = {}

        def rg(name):
            if name not in RG:
                RG[name] = Rg()
            return RG[name]

        for dst, src in ((ident, ident_h), (onesrow, onesrow_h), (onescol, onescol_h),
                         (gmask, gmask_h), (indic2, indic2_h), (epsc, epsc_h)):
            rg(dst.name).write(p, "sync")
            t = p.run("sync", lambda e, d=dst, s=src: e.dma_start(out=d[:], in_=s.ap()), dsem="pre")
            rg(dst.name).noted_write(t)
        p.barrier()

        pc = ExitStack()
        pcb = [sb("pcb%d" % i, [128, 8192], BF16, pc) for i in range(2)]
        pcb_rg = [Rg(), Rg()]
        for i in range(20):
            bi = i % 2
            pcb_rg[bi].write(p, "gpsimd")
            if i < 16:
                t = p.run("gpsimd", lambda e, f=i, b=bi: e.dma_start(
                    out=pcb[b][:, 0:E], in_=bass.AP(wq_h, f * 128 * E, [[E, 128], [1, E]])),
                    dsem="pcl%d" % bi)
            else:
                t = p.run("gpsimd", lambda e, o=(i - 16), b=bi: e.dma_start(
                    out=pcb[b][:], in_=bass.AP(wo_h, o * 128 * 8192, [[8192, 128], [1, 8192]])),
                    dsem="pcl%d" % bi)
            pcb_rg[bi].noted_write(t)
            pcb_rg[bi].read(p, "sync")
            if i < 16:
                t = p.run("sync", lambda e, f=i, b=bi: e.dma_start(
                    out=bass.AP(wqs_h, f * 128 * E, [[E, 128], [1, E]]), in_=pcb[b][:, 0:E]),
                    dsem="pcw%d" % bi)
            else:
                t = p.run("sync", lambda e, o=(i - 16), b=bi: e.dma_start(
                    out=bass.AP(wos_h, o * 128 * 8192, [[8192, 128], [1, 8192]]), in_=pcb[b][:]),
                    dsem="pcw%d" % bi)
            pcb_rg[bi].noted_read(t)
        p.barrier()
        pc.close()

        qs_stack = ExitStack()
        QPAR = 3
        wbb = [sb("wbb%d" % i, [128, 1024], BF16, qs_stack) for i in range(QPAR)]
        ubq = [sb("ubq%d" % i, [128, 1024], BF16, qs_stack) for i in range(QPAR)]
        sbt = [sb("sbt%d" % i, [128, 1024], BF16, qs_stack) for i in range(QPAR)]
        wbb_rg = [Rg() for _ in range(QPAR)]
        ub_rg = [Rg() for _ in range(QPAR)]
        sbt_rg = [Rg() for _ in range(QPAR)]
        wbb2 = [sb("q2w%d" % i, [128, 1024], BF16, qs_stack) for i in range(QPAR)]
        sb2 = [sb("q2s%d" % i, [128, 1024], BF16, qs_stack) for i in range(QPAR)]
        stg_hi = [sb("stg_hi%d" % i, [2, 2 * 1024], BF16, qs_stack) for i in range(2)]
        stg_lo = [sb("stg_lo%d" % i, [2, 2 * 1024], BF16, qs_stack) for i in range(2)]
        tcb = [sb("q2t%d" % i, [128, 1024], BF16, qs_stack) for i in range(QPAR)]
        Abuf = [sb("q2A%d" % i, [128, 1024], BF16, qs_stack) for i in range(QPAR)]
        Bbuf = [sb("q2B%d" % i, [128, 1024], BF16, qs_stack) for i in range(QPAR)]
        qsb = [sb("q2q%d" % i, [128, 1024], BF16, qs_stack) for i in range(QPAR)]
        qpk = sb("qpk", [64, 512], F32, qs_stack)
        qrs = sb("qrs", [64, 512], F32, qs_stack)
        qhi = sb("qhi", [64, 512], BF16, qs_stack)
        qdl = sb("qdl", [64, 512], F32, qs_stack)
        qlo = sb("qlo", [64, 512], BF16, qs_stack)
        w2_rg = [Rg() for _ in range(QPAR)]
        s2_rg = [Rg() for _ in range(QPAR)]
        stg_rg = [[Rg(), Rg()], [Rg(), Rg()]]
        tc_rg = [Rg() for _ in range(QPAR)]
        A_rg = [Rg() for _ in range(QPAR)]
        B_rg = [Rg() for _ in range(QPAR)]
        qs_rg = [Rg() for _ in range(QPAR)]
        qp_rg = Rg()

        def q_tiles():
            ti = 0
            for dc in range(16):
                for hgr in range(8):
                    yield ti, 0, dc, hgr
                    ti += 1
            for hc in range(64):
                for dgr in range(2):
                    yield ti, 1, hc, dgr
                    ti += 1

        def emit_q1(ti, mat, r0, c0):
            bi = ti % QPAR
            src_h = wfcT_h if mat == 0 else wprT_h
            ncols = HID if mat == 0 else E
            wbb_rg[bi].write(p, "gpsimd")
            t = p.run("gpsimd", lambda e, s=src_h, r=r0, c=c0, w=ncols, b=bi:
                      e.dma_start(out=wbb[b][:],
                                  in_=bass.AP(s, r * 128 * w + c * 1024, [[w, 128], [1, 1024]])),
                      dsem="wbl%d" % bi)
            wbb_rg[bi].noted_write(t)
            wbb_rg[bi].read(p, "scalar")
            ub_rg[bi].write(p, "scalar")
            t = p.run("scalar", lambda e, b=bi: e.activation(ubq[b][:], wbb[b][:], AF.Abs))
            wbb_rg[bi].noted_read(t)
            ub_rg[bi].noted_write(t)
            ub_rg[bi].read(p, "tensor")
            rg("gmaskk").read(p, "tensor")
            ps_rg[2 * bi].write(p, "tensor")
            ps_rg[2 * bi + 1].write(p, "tensor")

            def q1mm(e, b=bi):
                e.matmul(bank(2 * b), gmask[:], ubq[b][:, 0:512], start=True, stop=True)
                return e.matmul(bank(2 * b + 1), gmask[:], ubq[b][:, 512:1024], start=True, stop=True)
            t = p.run("tensor", q1mm)
            ps_rg[2 * bi].noted_write(t)
            ps_rg[2 * bi + 1].noted_write(t)
            ub_rg[bi].noted_read(t)
            ps_rg[2 * bi].read(p, "vector")
            ps_rg[2 * bi + 1].read(p, "vector")
            sbt_rg[bi].write(p, "vector")
            t = p.run("vector", lambda e, b=bi: e.tensor_single_scalar(
                sbt[b][:], PS[:, 2 * b * 512: (2 * b + 2) * 512], 1e-8, ALU.max))
            ps_rg[2 * bi].noted_read(t)
            ps_rg[2 * bi + 1].noted_read(t)
            sbt_rg[bi].noted_write(t)
            sbt_rg[bi].write(p, "vector")
            t = p.run("vector", lambda e, b=bi: e.tensor_copy(
                sbt[b][:].bitcast(U16), sbt[b][:].bitcast(U16)))
            sbt_rg[bi].noted_write(t)
            sbt_rg[bi].read(p, "sync")
            t = p.run("sync", lambda e, ti_=ti, b=bi: e.dma_start(
                out=bass.AP(ssc_h, 2 * ti_ * 1024, [[1024, 2], [1, 1024]]),
                in_=bass.AP(sbt[b], 0, [[64 * 1024, 2], [1, 1024]])), dsem="ssw%d" % bi)
            sbt_rg[bi].noted_read(t)
            sbt_rg[bi].read(p, "sync")
            t = p.run("sync", lambda e, ti_=ti, b=bi: e.dma_start(
                out=bass.AP(sbb_h, ti_ * 128 * 1024, [[1024, 128], [1, 1024]]),
                in_=sbt[b][:]), dsem="sbw%d" % bi)
            sbt_rg[bi].noted_read(t)
            wbb_rg[bi].read(p, "sync")
            t = p.run("sync", lambda e, s=(wb1_h if mat == 0 else wb2_h), r=r0, c=c0, w=ncols, b=bi:
                      e.dma_start(out=bass.AP(s, r * 128 * w + c * 1024, [[w, 128], [1, 1024]]),
                                  in_=wbb[b][:]), dsem="wbw%d" % bi)
            wbb_rg[bi].noted_read(t)

        def emit_qp(ch):
            for i_ in range(QPAR):
                p.wait("gpsimd", "ssw%d" % i_, p.counts.get("ssw%d" % i_, 0))
            qp_rg.write(p, "gpsimd")
            t = p.run("gpsimd", lambda e, c=ch: e.dma_start(
                out=qpk[:], in_=bass.AP(ssc_h, c * 32 * 1024, [[512, 64], [1, 512]])), dsem="qpl")
            qp_rg.noted_write(t)
            p.wait("vector", *t)
            t0 = p.run("vector", lambda e: e.tensor_single_scalar(qpk[:], qpk[:], 1e-8, ALU.max))
            p.wait("vector", *t0)
            t1 = p.run("vector", lambda e: e.reciprocal(qrs[:], qpk[:]))
            p.wait("vector", *t1)
            t2 = p.run("vector", lambda e: e.tensor_copy(qhi[:], qrs[:]))
            p.wait("vector", *t2)
            t3 = p.run("vector", lambda e: e.tensor_tensor(qdl[:], qrs[:], qhi[:], ALU.subtract))
            p.wait("vector", *t3)
            tv = p.run("vector", lambda e: e.tensor_copy(qlo[:], qdl[:]))
            qp_rg.noted_write(tv)
            p.wait("sync", *tv)
            t = p.run("sync", lambda e, c=ch: e.dma_start(
                out=bass.AP(rhi_h, c * 32 * 1024, [[512, 64], [1, 512]]), in_=qhi[:]), dsem="qpw")
            qp_rg.noted_read(t)
            t = p.run("sync", lambda e, c=ch: e.dma_start(
                out=bass.AP(rlo_h, c * 32 * 1024, [[512, 64], [1, 512]]), in_=qlo[:]), dsem="qpw")
            qp_rg.noted_read(t)

        def emit_q2(ti, mat, r0, c0):
            if ti == 0:
                sl_ = ["qpw"] + ["sbw%d" % i_ for i_ in range(QPAR)] + ["wbw%d" % i_ for i_ in range(QPAR)]
                for engx in ("sync", "gpsimd"):
                    for s_ in sl_:
                        p.wait(engx, s_, p.counts.get(s_, 0))
            bi = ti % QPAR
            sp = (ti // 2) % 2
            if ti % 2 == 0:
                for si, (stg, srg, sc) in enumerate(((stg_hi, stg_rg[0], rhi_h), (stg_lo, stg_rg[1], rlo_h))):
                    srg[sp].write(p, "sync")
                    t = p.run("sync", lambda e, d=stg[sp], s=sc, ti_=ti: e.dma_start(
                        out=d[:], in_=bass.AP(s, 2 * ti_ * 1024, [[1024, 2], [2048, 2], [1, 1024]])),
                        dsem="stg%d_%d" % (si, sp))
                    srg[sp].noted_write(t)
            off = (ti % 2) * 1024
            src_h = wb1_h if mat == 0 else wb2_h
            ncols = HID if mat == 0 else E
            w2_rg[bi].write(p, "sync")
            t = p.run("sync", lambda e, s=src_h, r=r0, c=c0, w=ncols, b=bi: e.dma_start(
                out=wbb2[b][:], in_=bass.AP(s, r * 128 * w + c * 1024, [[w, 128], [1, 1024]])),
                dsem="w2l%d" % bi)
            w2_rg[bi].noted_write(t)
            s2_rg[bi].write(p, "sync")
            t = p.run("sync", lambda e, ti_=ti, b=bi: e.dma_start(
                out=sb2[b][:], in_=bass.AP(sbb_h, ti_ * 128 * 1024, [[1024, 128], [1, 1024]])),
                dsem="sbl%d" % bi)
            s2_rg[bi].noted_write(t)
            stg_rg[0][sp].read(p, "tensor")
            stg_rg[1][sp].read(p, "tensor")
            ps_rg[2 * bi].write(p, "tensor")
            ps_rg[2 * bi + 1].write(p, "tensor")

            def mmrs(e, o=off, b=bi, s_=sp):
                e.matmul(bank(2 * b), indic2[:], stg_hi[s_][:, o:o + 512], start=True, stop=False)
                e.matmul(bank(2 * b), indic2[:], stg_lo[s_][:, o:o + 512], start=False, stop=True)
                e.matmul(bank(2 * b + 1), indic2[:], stg_hi[s_][:, o + 512:o + 1024], start=True, stop=False)
                return e.matmul(bank(2 * b + 1), indic2[:], stg_lo[s_][:, o + 512:o + 1024], start=False, stop=True)
            t = p.run("tensor", mmrs)
            ps_rg[2 * bi].noted_write(t)
            ps_rg[2 * bi + 1].noted_write(t)
            stg_rg[0][sp].noted_read(t)
            stg_rg[1][sp].noted_read(t)
            w2_rg[bi].read(p, "vector")
            ps_rg[2 * bi].read(p, "vector")
            ps_rg[2 * bi + 1].read(p, "vector")
            tc_rg[bi].write(p, "vector")
            t = p.run("vector", lambda e, b=bi: e.tensor_tensor(
                tcb[b][:], wbb2[b][:], PS[:, 2 * b * 512: (2 * b + 2) * 512], ALU.mult))
            w2_rg[bi].noted_read(t)
            ps_rg[2 * bi].noted_read(t)
            ps_rg[2 * bi + 1].noted_read(t)
            tc_rg[bi].noted_write(t)
            tc_rg[bi].write(p, "vector")
            t = p.run("vector", lambda e, b=bi: e.tensor_copy(
                tcb[b][:].bitcast(U16), tcb[b][:].bitcast(U16)))
            tc_rg[bi].noted_write(t)
            tc_rg[bi].read(p, "vector")
            s2_rg[bi].read(p, "vector")
            A_rg[bi].write(p, "vector")
            t = p.run("vector", lambda e, b=bi: e.scalar_tensor_tensor(
                Abuf[b][:], tcb[b][:], 0.5, sb2[b][:], ALU.is_gt, ALU.mult))
            A_rg[bi].noted_write(t)
            B_rg[bi].write(p, "vector")
            t = p.run("vector", lambda e, b=bi: e.scalar_tensor_tensor(
                Bbuf[b][:], tcb[b][:], -0.5, sb2[b][:], ALU.is_lt, ALU.mult))
            tc_rg[bi].noted_read(t)
            s2_rg[bi].noted_read(t)
            B_rg[bi].noted_write(t)
            A_rg[bi].read(p, "vector")
            B_rg[bi].read(p, "vector")
            qs_rg[bi].write(p, "vector")
            t = p.run("vector", lambda e, b=bi: e.tensor_tensor(qsb[b][:], Abuf[b][:], Bbuf[b][:], ALU.subtract))
            A_rg[bi].noted_read(t)
            B_rg[bi].noted_read(t)
            qs_rg[bi].noted_write(t)
            qs_rg[bi].read(p, "gpsimd")
            if mat == 0:
                t = p.run("gpsimd", lambda e, o=(2 * c0) * 128 * 8192 + r0 * 512, b=bi:
                          e.dma_start(out=bass.AP(w1q_h, o, [[8192, 128], [128 * 8192, 2], [1, 512]]),
                                      in_=qsb[b][:]), dsem="qqw%d" % bi)
            else:
                t = p.run("gpsimd", lambda e, o=(4 * c0) * 128 * 16384 + r0 * 256, b=bi:
                          e.dma_start(out=bass.AP(w2q_h, o, [[16384, 128], [128 * 16384, 4], [1, 256]]),
                                      in_=qsb[b][:]), dsem="qqw%d" % bi)
            qs_rg[bi].noted_read(t)

        import functools
        quant_units = [functools.partial(emit_q1, *a) for a in q_tiles()]
        quant_units += [functools.partial(emit_qp, ch) for ch in range(16)]
        quant_units += [functools.partial(emit_q2, *a) for a in q_tiles()]
        quant_units.reverse()

        def drain_quant(k=1):
            for _ in range(k):
                if quant_units:
                    quant_units.pop()()

        while quant_units:
            drain_quant(8)
        p.barrier()
        qs_stack.close()

        ab = ExitStack()
        cmask = sb("cmaskk", [128, 4 * 512], BF16, ab)
        ropec = sb("ropecc", [128, L], F32, ab)
        ropes = sb("ropess", [128, L], F32, ab)
        bq_t = sb("bq_tt", [128, 16], F32, ab)
        bk_t = sb("bk_tt", [128, 2], F32, ab)
        bvrow = sb("bvroww", [1, 256], F32, ab)
        bvh = sb("bvhh", [1, 256], BF16, ab)
        bvl = sb("bvll", [1, 256], BF16, ab)
        vbias = sb("vbiass", [128, 256], F32, ab)
        wkb = sb("wkbb", [128, 2 * E], BF16, ab)
        wvb = sb("wvbb", [128, 16 * 256], BF16, ab)
        for dst_, src_ in ((cmask, cmask_h), (bq_t, bq_h), (bk_t, bk_h)):
            rg(dst_.name).write(p, "sync")
            t = p.run("sync", lambda e, d=dst_, s=src_: e.dma_start(out=d[:], in_=s.ap()), dsem="pre2")
            rg(dst_.name).noted_write(t)
        for dst_, src_ in ((wvb, wv_h),):
            rg(dst_.name).write(p, "gpsimd")
            t = p.run("gpsimd", lambda e, d=dst_, s=src_: e.dma_start(out=d[:], in_=s.ap()), dsem="preg2")
            rg(dst_.name).noted_write(t)
        for dst_, src_ in ((ropec, ropec_h), (ropes, ropes_h), (bvrow, bv_h)):
            rg(dst_.name).write(p, "sync")
            t = p.run("sync", lambda e, d=dst_, s=src_: e.dma_start(out=d[:], in_=s.ap()), dsem="pre2")
            rg(dst_.name).noted_write(t)
        rg("wkbb").write(p, "gpsimd")
        t = p.run("gpsimd", lambda e: e.dma_start(
            out=wkb[:], in_=bass.AP(wk_h, 0, [[E, 128], [128 * E, 2], [1, E]])), dsem="preg2")
        rg("wkbb").noted_write(t)
        p.barrier()
        rg("bvroww").read(p, "vector")
        rg("bvhh").write(p, "vector")
        t = p.run("vector", lambda e: e.tensor_copy(bvh[:], bvrow[:]))
        rg("bvhh").noted_write(t)
        rg("bvll").write(p, "vector")
        rg("bvhh").read(p, "vector")
        t = p.run("vector", lambda e: e.tensor_tensor(bvl[:], bvrow[:], bvh[:], ALU.subtract))
        rg("bvroww").noted_read(t)
        rg("bvhh").noted_read(t)
        rg("bvll").noted_write(t)
        rg("bvhh").read(p, "tensor")
        rg("bvll").read(p, "tensor")
        rg("onesroww").read(p, "tensor")
        ps_rg[0].write(p, "tensor")

        def bvmm(e):
            e.matmul(PS[:, 0:256], onesrow[:], bvh[:], start=True, stop=False)
            return e.matmul(PS[:, 0:256], onesrow[:], bvl[:], start=False, stop=True)
        t = p.run("tensor", bvmm)
        ps_rg[0].noted_write(t)
        ps_rg[0].read(p, "scalar")
        rg("vbiass").write(p, "scalar")
        t = p.run("scalar", lambda e: e.activation(vbias[:], PS[:, 0:256], AF.Copy))
        ps_rg[0].noted_read(t)
        rg("vbiass").noted_write(t)
        xa = sb("xa", [128, E], F32, ab)
        trash = [sb("trasha0", [128, E], BF16, ab), sb("trasha1", [128, E], BF16, ab)]
        stat = [sb("stata0", [128, 8], F32, ab), sb("stata1", [128, 8], F32, ab)]
        xln1t = sb("xln1t", [128, 16 * 512], BF16, ab)
        wqb = [sb("wqb0", [128, E], BF16, ab), sb("wqb1", [128, E], BF16, ab)]
        qevac = [sb("qevac0", [128, 512], BF16, ab), sb("qevac1", [128, 512], BF16, ab)]
        qswap = [sb("qswap0", [128, 512], BF16, ab), sb("qswap1", [128, 512], BF16, ab)]
        ropem = [sb("ropem0", [128, 512], F32, ab), sb("ropem1", [128, 512], F32, ab)]
        qtq = sb("qtq", [128, 16 * 512], BF16, ab)
        ktb = sb("ktb", [128, 2 * L], BF16, ab)
        vtb = sb("vtb", [128, 16 * 256], BF16, ab)
        ptb = [sb("ptb0", [128, 512], BF16, ab), sb("ptb1", [128, 512], BF16, ab)]
        srow = sb("srow", [1, 512], F32, ab)
        rhirow = sb("rhirow", [1, 512], BF16, ab)
        rlorow = sb("rlorow", [1, 512], BF16, ab)
        rbs = [sb("rbs0", [128, 512], F32, ab), sb("rbs1", [128, 512], F32, ab)]
        avt = sb("avt", [128, 16 * 512], BF16, ab)
        wob = [sb("wob0", [128, 16 * 512], BF16, ab)]
        xr = [sb("xr0", [128, 512], F32, ab), sb("xr1", [128, 512], F32, ab)]
        x2st = [sb("x2st0", [128, 512], F32, ab), sb("x2st1", [128, 512], F32, ab)]

        xa_rg = Rg(); xln_rg = Rg(); wq_rg = [Rg(), Rg()]; qe_rg = [Rg(), Rg()]
        qt_rg = Rg(); kt_rg = Rg(); vt_rg = Rg()
        pt_rg = [Rg(), Rg()]; row_rg = Rg(); rbs_rg = [Rg(), Rg()]; avt_rg = Rg()
        wo_rg = [Rg(), Rg()]; xr_rg = [Rg(), Rg()]; x2_rg = [Rg(), Rg()]
        stat_rg = [Rg(), Rg()]; trash_rg = [Rg(), Rg()]

        def ln_into(xsrc_h, row0, xabuf, xarg, trbuf, trrg, stbuf, strg, xsem):
            xarg.write(p, "sync")
            t = p.run("sync", lambda e: e.dma_start(
                out=xabuf[:], in_=bass.AP(xsrc_h, row0 * E, [[E, 128], [1, E]])), dsem=xsem)
            xarg.noted_write(t)
            xarg.read(p, "vector")
            strg.write(p, "vector")
            t = p.run("vector", lambda e: e.memset(stbuf[:, 1:2], 0.0))
            strg.noted_write(t)
            strg.write(p, "vector")
            xarg.read(p, "vector")
            t = p.run("vector", lambda e: e.reduce_sum(stbuf[:, 0:1], xabuf[:], axis=AX.X))
            xarg.noted_read(t)
            strg.noted_write(t)
            xarg.read(p, "scalar")
            strg.write(p, "scalar")
            trrg.write(p, "scalar")
            t = p.run("scalar", lambda e: e.activation(
                trbuf[:], xabuf[:], AF.Square, accum_out=stbuf[:, 1:2]))
            xarg.noted_read(t)
            trrg.noted_write(t)
            strg.noted_write(t)
            for fn in (
                lambda e: e.tensor_scalar(stbuf[:, 2:3], stbuf[:, 0:1], 1.0 / E, None, ALU.mult),
                lambda e: e.tensor_scalar(stbuf[:, 3:4], stbuf[:, 1:2], 1.0 / E, None, ALU.mult),
                lambda e: e.scalar_tensor_tensor(stbuf[:, 4:5], stbuf[:, 2:3], -1.0, stbuf[:, 2:3],
                                                 ALU.mult, ALU.mult),
                lambda e: e.tensor_tensor(stbuf[:, 4:5], stbuf[:, 3:4], stbuf[:, 4:5], ALU.add),
            ):
                strg.write(p, "vector")
                t = p.run("vector", fn)
                strg.noted_write(t)
            strg.write(p, "scalar")
            t = p.run("scalar", lambda e: e.activation(stbuf[:, 5:6], stbuf[:, 4:5], AF.Ln, bias=epsc[:, 0:1]))
            strg.noted_write(t)
            strg.write(p, "scalar")
            t = p.run("scalar", lambda e: e.activation(stbuf[:, 6:7], stbuf[:, 5:6], AF.Exp, scale=-0.5))
            strg.noted_write(t)
            strg.write(p, "vector")
            t = p.run("vector", lambda e: e.scalar_tensor_tensor(
                stbuf[:, 7:8], stbuf[:, 2:3], -1.0, stbuf[:, 6:7], ALU.mult, ALU.mult))
            strg.noted_write(t)
            xarg.read(p, "scalar")
            strg.read(p, "scalar")
            trrg.write(p, "scalar")
            t = p.run("scalar", lambda e: e.activation(
                trbuf[:], xabuf[:], AF.Identity, bias=stbuf[:, 7:8], scale=stbuf[:, 6:7]))
            xarg.noted_read(t)
            strg.noted_read(t)
            trrg.noted_write(t)

        def transpose_into(tsub, trbuf, trrg, dstbuf, dstrg):
            for grp in range(2):
                trrg.read(p, "tensor")
                psb_rg[grp].write(p, "tensor")

                def tp(e, g=grp):
                    for j in range(8):
                        c = g * 8 + j
                        ins = e.transpose(PSB[:, g * 1024 + j * 128: g * 1024 + (j + 1) * 128],
                                          trbuf[:, c * 128:(c + 1) * 128], ident[:])
                    return ins
                t = p.run("tensor", tp)
                psb_rg[grp].noted_write(t)
                trrg.noted_read(t)
                psb_rg[grp].read(p, "vector")
                dstrg.write(p, "vector")

                def ev(e, g=grp, ts=tsub):
                    return e.tensor_copy(
                        dstbuf[:].rearrange("p (dc t) -> p dc t", t=512)
                        [:, g * 8:(g + 1) * 8, ts * 128:(ts + 1) * 128],
                        PSB[:, g * 1024:(g + 1) * 1024].rearrange("p (j t) -> p j t", t=128))
                t = p.run("vector", ev)
                psb_rg[grp].noted_read(t)
                dstrg.noted_write(t)

        qsw_rg = [Rg(), Rg()]
        rm_rg = [Rg(), Rg()]

        def rope_into(qt_, dstbuf, dstrg, d0, par):
            qe = qevac[par]
            qs_ = qswap[par]
            qe_rg[par].read(p, "gpsimd")
            qsw_rg[par].write(p, "gpsimd")
            t = p.run("gpsimd", lambda e: e.dma_start(out=qs_[0:64, :], in_=qe[64:128, :]), dsem="swp%d" % par)
            qsw_rg[par].noted_write(t)
            t = p.run("gpsimd", lambda e: e.dma_start(out=qs_[64:128, :], in_=qe[0:64, :]), dsem="swp%d" % par)
            qsw_rg[par].noted_write(t)
            qe_rg[par].noted_read(t)
            qe_rg[par].read(p, "vector")
            qsw_rg[par].read(p, "vector")
            rg("ropecc").read(p, "vector")
            rg("ropess").read(p, "vector")
            dstrg.write(p, "vector")
            qt__, d0_ = qt_, d0
            csl = ropec[0:64, qt__ * 512:(qt__ + 1) * 512]
            snl = ropes[0:64, qt__ * 512:(qt__ + 1) * 512]
            csh = ropec[64:128, qt__ * 512:(qt__ + 1) * 512]
            snh = ropes[64:128, qt__ * 512:(qt__ + 1) * 512]
            rm_rg[0].write(p, "vector")
            t = p.run("vector", lambda e, a=csl, q_=qe: e.tensor_tensor(ropem[0][0:64, :], q_[0:64, :], a, ALU.mult))
            rm_rg[0].noted_write(t)
            rm_rg[1].write(p, "vector")
            t = p.run("vector", lambda e, a=snl, q_=qs_: e.tensor_tensor(ropem[1][0:64, :], q_[0:64, :], a, ALU.mult))
            rm_rg[1].noted_write(t)
            rm_rg[0].read(p, "vector")
            rm_rg[1].read(p, "vector")
            dstrg.write(p, "vector")
            t = p.run("vector", lambda e, d=d0_, db=dstbuf: e.tensor_tensor(
                db[0:64, d:d + 512], ropem[0][0:64, :], ropem[1][0:64, :], ALU.subtract))
            rm_rg[0].noted_read(t)
            rm_rg[1].noted_read(t)
            dstrg.noted_write(t)
            rm_rg[0].write(p, "vector")
            qe_rg[par].read(p, "vector")
            t = p.run("vector", lambda e, a=csh, q_=qe: e.tensor_tensor(ropem[0][64:128, :], q_[64:128, :], a, ALU.mult))
            rm_rg[0].noted_write(t)
            qe_rg[par].noted_read(t)
            rm_rg[1].write(p, "vector")
            qsw_rg[par].read(p, "vector")
            t = p.run("vector", lambda e, a=snh, q_=qs_: e.tensor_tensor(ropem[1][64:128, :], q_[64:128, :], a, ALU.mult))
            rm_rg[1].noted_write(t)
            qsw_rg[par].noted_read(t)
            rm_rg[0].read(p, "vector")
            rm_rg[1].read(p, "vector")
            dstrg.write(p, "vector")
            t = p.run("vector", lambda e, d=d0_, db=dstbuf: e.tensor_tensor(
                db[64:128, d:d + 512], ropem[0][64:128, :], ropem[1][64:128, :], ALU.add))
            rm_rg[0].noted_read(t)
            rm_rg[1].noted_read(t)
            dstrg.noted_write(t)

        for b in range(B_PER):
            for qt in range(4):
                nkc = 4 * (qt + 1)
                for tsub in range(4):
                    row0 = b * L + qt * 512 + tsub * 128
                    tp_ = tsub % 2
                    ln_into(x_h, row0, xa, xa_rg, trash[tp_], trash_rg[tp_], stat[tp_], stat_rg[tp_], "xal")
                    transpose_into(tsub, trash[tp_], trash_rg[tp_], xln1t, xln_rg)
                    drain_quant(2)
                for fc in range(HQ):
                    bi = fc % 2
                    wq_rg[bi].write(p, "sync")
                    t = p.run("sync", lambda e, f=fc, b_=bi: e.dma_start(
                        out=wqb[b_][:], in_=bass.AP(wqs_h, f * 128 * E, [[E, 128], [1, E]])),
                        dsem="wql%d" % bi)
                    wq_rg[bi].noted_write(t)
                    pb = fc % 2
                    wq_rg[bi].read(p, "tensor")
                    xln_rg.read(p, "tensor")
                    ps_rg[pb].write(p, "tensor")

                    def qmm(e, b_=bi, pb_=pb):
                        for dc in range(16):
                            ins = e.matmul(bank(pb_), wqb[b_][:, dc * 128:(dc + 1) * 128],
                                           xln1t[:, dc * 512:(dc + 1) * 512],
                                           start=(dc == 0), stop=(dc == 15))
                        return ins
                    t = p.run("tensor", qmm)
                    ps_rg[pb].noted_write(t)
                    wq_rg[bi].noted_read(t)
                    xln_rg.noted_read(t)
                    ps_rg[pb].read(p, "scalar")
                    rg("bq_tt").read(p, "scalar")
                    qe_rg[bi].write(p, "scalar")
                    t = p.run("scalar", lambda e, pb_=pb, f=fc, b_=bi: e.activation(
                        qevac[b_][:], bank(pb_), AF.Identity, bias=bq_t[:, f:f + 1]))
                    ps_rg[pb].noted_read(t)
                    qe_rg[bi].noted_write(t)
                    rope_into(qt, qtq, qt_rg, fc * 512, bi)
                    drain_quant(2)
                for kv in range(HKV):
                    pb = kv % 2
                    rg("wkbb").read(p, "tensor")
                    xln_rg.read(p, "tensor")
                    ps_rg[pb].write(p, "tensor")

                    def kmm(e, k=kv, pb_=pb):
                        for dc in range(16):
                            ins = e.matmul(bank(pb_), wkb[:, k * E + dc * 128: k * E + (dc + 1) * 128],
                                           xln1t[:, dc * 512:(dc + 1) * 512],
                                           start=(dc == 0), stop=(dc == 15))
                        return ins
                    t = p.run("tensor", kmm)
                    ps_rg[pb].noted_write(t)
                    xln_rg.noted_read(t)
                    ps_rg[pb].read(p, "scalar")
                    rg("bk_tt").read(p, "scalar")
                    qe_rg[kv].write(p, "scalar")
                    t = p.run("scalar", lambda e, pb_=pb, k=kv: e.activation(
                        qevac[k][:], bank(pb_), AF.Identity, bias=bk_t[:, k:k + 1]))
                    ps_rg[pb].noted_read(t)
                    qe_rg[kv].noted_write(t)
                    rope_into(qt, ktb, kt_rg, kv * L + qt * 512, kv)
                    drain_quant(1)
                for tsub in range(4):
                    pb = 2
                    rg("wvbb").read(p, "tensor")
                    xln_rg.read(p, "tensor")
                    ps_rg[pb].write(p, "tensor")

                    def vmm(e, ts=tsub, pb_=pb):
                        for dc in range(16):
                            ins = e.matmul(PS[:, pb_ * 512: pb_ * 512 + 256],
                                           xln1t[:, dc * 512 + ts * 128: dc * 512 + (ts + 1) * 128],
                                           wvb[:, dc * 256:(dc + 1) * 256],
                                           start=(dc == 0), stop=(dc == 15))
                        return ins
                    t = p.run("tensor", vmm)
                    ps_rg[pb].noted_write(t)
                    xln_rg.noted_read(t)
                    ps_rg[pb].read(p, "vector")
                    rg("vbiass").read(p, "vector")
                    vt_rg.write(p, "vector")
                    t = p.run("vector", lambda e, pb_=pb, ts=tsub, q_=qt: e.tensor_tensor(
                        vtb[:, (q_ * 4 + ts) * 256:(q_ * 4 + ts + 1) * 256],
                        PS[:, pb_ * 512: pb_ * 512 + 256], vbias[:], ALU.add))
                    ps_rg[pb].noted_read(t)
                    vt_rg.noted_write(t)
                    drain_quant(1)
                pending_tail = [None, None]
                for kv in range(HKV):
                    for g in range(8):
                        h = kv * 8 + g
                        hp = h % 2
                        avb = 2 if hp == 0 else 4
                        smb = 3 if hp == 0 else 5
                        def emit_S(kc):
                            o = kc - 4 * qt
                            sbk = kc % 2
                            kt_rg.read(p, "tensor")
                            qt_rg.read(p, "tensor")
                            ps_rg[sbk].write(p, "tensor")

                            def smm(e, k=kv, h_=h, kc_=kc, o_=o, sbk_=sbk):
                                ins = e.matmul(bank(sbk_),
                                               ktb[:, k * L + kc_ * 128: k * L + (kc_ + 1) * 128],
                                               qtq[:, h_ * 512:(h_ + 1) * 512],
                                               start=True, stop=(o_ < 0))
                                if o_ >= 0:
                                    ins = e.matmul(bank(sbk_), ident[:],
                                                   cmask[:, o_ * 512:(o_ + 1) * 512],
                                                   start=False, stop=True)
                                return ins
                            t = p.run("tensor", smm)
                            ps_rg[sbk].noted_write(t)
                            kt_rg.noted_read(t)
                            qt_rg.noted_read(t)

                        emit_S(0)
                        for kc in range(nkc):
                            sbk = kc % 2
                            ps_rg[sbk].read(p, "scalar")
                            pt_rg[sbk].write(p, "scalar")
                            t = p.run("scalar", lambda e, sbk_=sbk: e.activation(
                                ptb[sbk_][:], bank(sbk_), AF.Exp, scale=SCALE))
                            ps_rg[sbk].noted_read(t)
                            pt_rg[sbk].noted_write(t)
                            if kc + 1 < nkc:
                                emit_S(kc + 1)
                            if kc == 0 and pending_tail[0] is not None:
                                pending_tail[0]()
                                pending_tail[0] = None
                            if kc == 3 and pending_tail[1] is not None:
                                pending_tail[1]()
                                pending_tail[1] = None
                            pt_rg[sbk].read(p, "tensor")
                            vt_rg.read(p, "tensor")
                            ps_rg[avb].write(p, "tensor")
                            ps_rg[smb].write(p, "tensor")

                            def avmm(e, k=kv, kc_=kc, sbk_=sbk, first=(kc == 0), last=(kc == nkc - 1),
                                     avb_=avb, smb_=smb):
                                e.matmul(bank(avb_),
                                         vtb[:, kc_ * 256 + k * 128: kc_ * 256 + k * 128 + 128],
                                         ptb[sbk_][:], start=first, stop=last)
                                return e.matmul(PS[0:1, smb_ * 512:(smb_ + 1) * 512], onescol[:], ptb[sbk_][:],
                                                start=first, stop=last)
                            t = p.run("tensor", avmm)
                            ps_rg[avb].noted_write(t)
                            ps_rg[smb].noted_write(t)
                            pt_rg[sbk].noted_read(t)
                            vt_rg.noted_read(t)
                        def tail1(h=h, hp=hp, avb=avb, smb=smb):
                            ps_rg[smb].read(p, "vector")
                            row_rg.write(p, "vector")
                            t = p.run("vector", lambda e, smb_=smb: e.tensor_copy(
                                srow[:], PS[0:1, smb_ * 512:(smb_ + 1) * 512]))
                            ps_rg[smb].noted_read(t)
                            row_rg.noted_write(t)
                            for fn in (lambda e: e.activation(srow[:], srow[:], AF.Ln),
                                       lambda e: e.activation(srow[:], srow[:], AF.Exp, scale=-1.0)):
                                row_rg.write(p, "scalar")
                                t = p.run("scalar", fn)
                                row_rg.noted_write(t)
                            row_rg.write(p, "vector")
                            t = p.run("vector", lambda e: e.tensor_copy(rhirow[:], srow[:]))
                            row_rg.noted_write(t)
                            row_rg.write(p, "vector")
                            t = p.run("vector", lambda e: e.tensor_tensor(
                                rlorow[:], srow[:], rhirow[:], ALU.subtract))
                            row_rg.noted_write(t)

                        def tail2(h=h, hp=hp, avb=avb, smb=smb):
                            row_rg.read(p, "tensor")
                            rg("onesroww").read(p, "tensor")
                            ps_rg[smb].write(p, "tensor")

                            def rbmm(e, smb_=smb):
                                e.matmul(bank(smb_), onesrow[:], rhirow[:], start=True, stop=False)
                                return e.matmul(bank(smb_), onesrow[:], rlorow[:], start=False, stop=True)
                            t = p.run("tensor", rbmm)
                            ps_rg[smb].noted_write(t)
                            row_rg.noted_read(t)
                            ps_rg[smb].read(p, "vector")
                            rbs_rg[hp].write(p, "vector")
                            t = p.run("vector", lambda e, smb_=smb, hp_=hp: e.tensor_copy(
                                rbs[hp_][:], bank(smb_)))
                            ps_rg[smb].noted_read(t)
                            rbs_rg[hp].noted_write(t)
                            ps_rg[avb].read(p, "vector")
                            rbs_rg[hp].read(p, "vector")
                            avt_rg.write(p, "vector")
                            t = p.run("vector", lambda e, h_=h, avb_=avb, hp_=hp: e.tensor_tensor(
                                avt[:, h_ * 512:(h_ + 1) * 512], bank(avb_), rbs[hp_][:], ALU.mult))
                            ps_rg[avb].noted_read(t)
                            rbs_rg[hp].noted_read(t)
                            avt_rg.noted_write(t)
                            drain_quant(2)
                        pending_tail[0] = tail1
                        pending_tail[1] = tail2
                for pi in range(2):
                    if pending_tail[pi] is not None:
                        pending_tail[pi]()
                        pending_tail[pi] = None
                for oq in range(4):
                    wp = 0
                    wo_rg[wp].write(p, "sync")
                    t = p.run("sync", lambda e, o=oq, w_=wp: e.dma_start(
                        out=wob[w_][:], in_=bass.AP(wos_h, o * 128 * 8192,
                                                    [[8192, 128], [1, 8192]])), dsem="wol%d" % wp)
                    wo_rg[wp].noted_write(t)
                    for tsub in range(4):
                        sl = tsub % 2
                        r0 = b * L + qt * 512 + tsub * 128
                        xr_rg[sl].write(p, "sync")
                        t = p.run("sync", lambda e, r_=r0, o=oq, sl_=sl: e.dma_start(
                            out=xr[sl_][:], in_=bass.AP(x_h, r_ * E + o * 512, [[E, 128], [1, 512]])),
                            dsem="xrl%d" % sl)
                        xr_rg[sl].noted_write(t)
                        pb = tsub % 2
                        avt_rg.read(p, "tensor")
                        wo_rg[wp].read(p, "tensor")
                        ps_rg[pb].write(p, "tensor")

                        def omm(e, ts=tsub, pb_=pb, w_=wp):
                            for dc in range(16):
                                ins = e.matmul(bank(pb_),
                                               avt[:, dc * 512 + ts * 128: dc * 512 + (ts + 1) * 128],
                                               wob[w_][:, dc * 512:(dc + 1) * 512],
                                               start=(dc == 0), stop=(dc == 15))
                            return ins
                        t = p.run("tensor", omm)
                        ps_rg[pb].noted_write(t)
                        avt_rg.noted_read(t)
                        wo_rg[wp].noted_read(t)
                        ps_rg[pb].read(p, "vector")
                        xr_rg[sl].read(p, "vector")
                        x2_rg[sl].write(p, "vector")
                        t = p.run("vector", lambda e, pb_=pb, sl_=sl: e.tensor_tensor(
                            x2st[sl_][:], bank(pb_), xr[sl_][:], ALU.add))
                        ps_rg[pb].noted_read(t)
                        xr_rg[sl].noted_read(t)
                        x2_rg[sl].noted_write(t)
                        x2_rg[sl].read(p, "gpsimd")
                        t = p.run("gpsimd", lambda e, r_=r0, o=oq, sl_=sl: e.dma_start(
                            out=bass.AP(x2_h, r_ * E + o * 512, [[E, 128], [1, 512]]),
                            in_=x2st[sl_][:]), dsem="x2w%d" % sl)
                        x2_rg[sl].noted_read(t)
                    drain_quant(1)
        p.barrier()
        ab.close()

        cc = ExitStack()
        xc = sb("xc", [128, E], F32, cc)
        trc = [sb("trc0", [128, E], BF16, cc), sb("trc1", [128, E], BF16, cc)]
        stc = [sb("stc0", [128, 8], F32, cc), sb("stc1", [128, 8], F32, cc)]
        xln2t = sb("xln2t", [128, 16 * 512], BF16, cc)
        w1b = [sb("w1b0", [128, 16 * 512], BF16, cc), sb("w1b1", [128, 16 * 512], BF16, cc)]
        relf = sb("relf", [128, 512], BF16, cc)
        h2t = sb("h2t", [128, 64 * 512], BF16, cc)
        dtmp = sb("dtmp", [128, 128], F32, cc)
        invb = sb("invb", [128, 8], F32, cc)
        w2b = [sb("w2b0", [128, 64 * 256], BF16, cc), sb("w2b1", [128, 64 * 256], BF16, cc)]
        x2r = [sb("x2r0", [128, 256], F32, cc), sb("x2r1", [128, 256], F32, cc)]
        outst = [sb("outst0", [128, 256], F32, cc), sb("outst1", [128, 256], F32, cc)]

        xc_rg = Rg(); trc_rg = [Rg(), Rg()]; stc_rg = [Rg(), Rg()]; xl2_rg = Rg(); dt_rg = Rg()
        w1_rg = [Rg(), Rg()]; rel_rg = Rg(); h2_rg = Rg()
        inv_rg = Rg(); w2q_rg = [Rg(), Rg()]; x2r_rg = [Rg(), Rg()]; out_rg = [Rg(), Rg()]

        for tile in range(8):
            base = tile * 512
            for tsub in range(4):
                tp_ = tsub % 2
                ln_into(x2_h, base + tsub * 128, xc, xc_rg, trc[tp_], trc_rg[tp_], stc[tp_], stc_rg[tp_], "xcl")
                transpose_into(tsub, trc[tp_], trc_rg[tp_], xln2t, xl2_rg)
            for hg in range(16):
                bi = hg % 2
                w1_rg[bi].write(p, "gpsimd")
                t = p.run("gpsimd", lambda e, h=hg, b_=bi: e.dma_start(
                    out=w1b[b_][:],
                    in_=bass.AP(w1q_h, h * 128 * 8192, [[8192, 128], [1, 8192]])), dsem="w1l%d" % bi)
                w1_rg[bi].noted_write(t)
                for hc4 in range(4):
                    hc = hg * 4 + hc4
                    pb = hc % 2
                    w1_rg[bi].read(p, "tensor")
                    xl2_rg.read(p, "tensor")
                    ps_rg[pb].write(p, "tensor")

                    def f1mm(e, b_=bi, h4=hc4, pb_=pb):
                        for dc in range(16):
                            ins = e.matmul(bank(pb_),
                                           w1b[b_][:, dc * 512 + h4 * 128: dc * 512 + (h4 + 1) * 128],
                                           xln2t[:, dc * 512:(dc + 1) * 512],
                                           start=(dc == 0), stop=(dc == 15))
                        return ins
                    t = p.run("tensor", f1mm)
                    ps_rg[pb].noted_write(t)
                    w1_rg[bi].noted_read(t)
                    xl2_rg.noted_read(t)
                    ps_rg[pb].read(p, "scalar")
                    rel_rg.write(p, "scalar")
                    t = p.run("scalar", lambda e, pb_=pb: e.activation(relf[:], bank(pb_), AF.Relu))
                    ps_rg[pb].noted_read(t)
                    rel_rg.noted_write(t)
                    rel_rg.read(p, "scalar")
                    h2_rg.write(p, "scalar")
                    t = p.run("scalar", lambda e, h_=hc: e.activation(
                        h2t[:, h_ * 512:(h_ + 1) * 512], relf[:], AF.Square))
                    rel_rg.noted_read(t)
                    h2_rg.noted_write(t)
                h2_rg.read(p, "tensor")
                ps_rg[2].write(p, "tensor")

                def dmm(e, hg_=hg):
                    for hc4 in range(4):
                        h_ = hg_ * 4 + hc4
                        for ts in range(4):
                            ins = e.matmul(PS[:, 2 * 512 + ts * 128: 2 * 512 + (ts + 1) * 128],
                                           h2t[:, h_ * 512 + ts * 128: h_ * 512 + (ts + 1) * 128],
                                           h2t[:, h_ * 512 + ts * 128: h_ * 512 + (ts + 1) * 128],
                                           start=(h_ == 0 and ts == 0), stop=(h_ == 63 and ts == 3))
                    return ins
                t = p.run("tensor", dmm)
                ps_rg[2].noted_write(t)
                h2_rg.noted_read(t)
            for tsub in range(4):
                ps_rg[2].read(p, "vector")
                dt_rg.write(p, "vector")
                t = p.run("vector", lambda e, ts=tsub: e.tensor_tensor(
                    dtmp[:], PS[:, 2 * 512 + ts * 128: 2 * 512 + (ts + 1) * 128], ident[:], ALU.mult))
                ps_rg[2].noted_read(t)
                dt_rg.noted_write(t)
                dt_rg.read(p, "vector")
                inv_rg.write(p, "vector")
                t = p.run("vector", lambda e, ts=tsub: e.reduce_sum(invb[:, ts:ts + 1], dtmp[:], axis=AX.X))
                dt_rg.noted_read(t)
                inv_rg.noted_write(t)
                inv_rg.write(p, "vector")
                t = p.run("vector", lambda e, ts=tsub: e.tensor_scalar(
                    invb[:, 4 + ts:5 + ts], invb[:, ts:ts + 1], 1.0 / HID, None, ALU.mult))
                inv_rg.noted_write(t)
                inv_rg.write(p, "scalar")
                t = p.run("scalar", lambda e, ts=tsub: e.activation(
                    invb[:, ts:ts + 1], invb[:, 4 + ts:5 + ts], AF.Ln, bias=epsc[:, 1:2]))
                inv_rg.noted_write(t)
                inv_rg.write(p, "scalar")
                t = p.run("scalar", lambda e, ts=tsub: e.activation(
                    invb[:, 4 + ts:5 + ts], invb[:, ts:ts + 1], AF.Exp, scale=-0.5))
                inv_rg.noted_write(t)
            for dq in range(8):
                wb_ = dq % 2
                w2q_rg[wb_].write(p, "gpsimd")
                t = p.run("gpsimd", lambda e, d=dq, b_=wb_: e.dma_start(
                    out=w2b[b_][:],
                    in_=bass.AP(w2q_h, d * 128 * 16384, [[16384, 128], [1, 16384]])),
                    dsem="w2s%d" % wb_)
                w2q_rg[wb_].noted_write(t)
                for tsub in range(4):
                    pb = 3 + (tsub % 2)
                    sl = tsub % 2
                    h2_rg.read(p, "tensor")
                    w2q_rg[wb_].read(p, "tensor")
                    ps_rg[pb].write(p, "tensor")

                    def f2mm(e, ts=tsub, pb_=pb, b_=wb_):
                        for hc in range(64):
                            ins = e.matmul(PS[:, pb_ * 512: pb_ * 512 + 256],
                                           h2t[:, hc * 512 + ts * 128: hc * 512 + (ts + 1) * 128],
                                           w2b[b_][:, hc * 256:(hc + 1) * 256],
                                           start=(hc == 0), stop=(hc == 63))
                        return ins
                    t = p.run("tensor", f2mm)
                    ps_rg[pb].noted_write(t)
                    h2_rg.noted_read(t)
                    w2q_rg[wb_].noted_read(t)
                    x2r_rg[sl].write(p, "sync")
                    t = p.run("sync", lambda e, ts=tsub, d=dq, b_=base, sl_=sl: e.dma_start(
                        out=x2r[sl_][:],
                        in_=bass.AP(x2_h, (b_ + ts * 128) * E + d * 256, [[E, 128], [1, 256]])),
                        dsem="x2r%d" % sl)
                    x2r_rg[sl].noted_write(t)
                    ps_rg[pb].read(p, "vector")
                    x2r_rg[sl].read(p, "vector")
                    inv_rg.read(p, "vector")
                    out_rg[sl].write(p, "vector")
                    t = p.run("vector", lambda e, ts=tsub, pb_=pb, sl_=sl: e.scalar_tensor_tensor(
                        outst[sl_][:], PS[:, pb_ * 512: pb_ * 512 + 256],
                        invb[:, 4 + ts:5 + ts], x2r[sl_][:], ALU.mult, ALU.add))
                    ps_rg[pb].noted_read(t)
                    x2r_rg[sl].noted_read(t)
                    inv_rg.noted_read(t)
                    out_rg[sl].noted_write(t)
                    out_rg[sl].read(p, "sync")
                    t = p.run("sync", lambda e, ts=tsub, d=dq, b_=base, sl_=sl: e.dma_start(
                        out=bass.AP(out_h, (b_ + ts * 128) * E + d * 256, [[E, 128], [1, 256]]),
                        in_=outst[sl_][:]), dsem="osw%d" % sl)
                    out_rg[sl].noted_read(t)
        cc.close()

        p.barrier()

        with nc.Block() as block:
            @block.sync
            def _(e):
                for fn in p.sched["sync"]:
                    fn(e)

            @block.gpsimd
            def _(e):
                for fn in p.sched["gpsimd"]:
                    fn(e)

            @block.tensor
            def _(e):
                for fn in p.sched["tensor"]:
                    fn(e)

            @block.vector
            def _(e):
                for fn in p.sched["vector"]:
                    fn(e)

            @block.scalar
            def _(e):
                for fn in p.sched["scalar"]:
                    fn(e)

    return nc, p


def _host_inputs(x, wq, wk, wv, wo, bq, bk, bv, W_fc, W_proj, core):
    xs = np.ascontiguousarray(x[core * B_PER:(core + 1) * B_PER].reshape(B_PER * L, E))
    d = {"x": xs}
    wqT = np.ascontiguousarray(wq.T)
    d["wq_fb"] = np.ascontiguousarray(
        wqT.reshape(16, 128, 16, 128).transpose(2, 1, 0, 3).reshape(HQ, 128, E))
    wkT = np.ascontiguousarray(wk.T)
    d["wk_fb"] = np.ascontiguousarray(
        wkT.reshape(16, 128, 2, 128).transpose(2, 1, 0, 3).reshape(HKV, 128, E))
    wvT = np.ascontiguousarray(wv.T)
    d["wv_rb"] = np.ascontiguousarray(wvT.reshape(16, 128, 256).transpose(1, 0, 2).reshape(128, 16 * 256))
    woT = np.ascontiguousarray(wo.T)
    wo_q = np.empty((4, 128, 16 * 512), np.float32)
    for oq in range(4):
        blk = woT[:, oq * 512:(oq + 1) * 512]
        wo_q[oq] = blk.reshape(16, 128, 512).transpose(1, 0, 2).reshape(128, 16 * 512)
    d["wo_qb"] = wo_q
    d["bq_t"] = np.ascontiguousarray(bq.reshape(16, 128).T)
    d["bk_t"] = np.ascontiguousarray(bk.reshape(2, 128).T)
    d["bv_row"] = np.ascontiguousarray(bv.reshape(1, 256))
    d["wfcT"] = np.ascontiguousarray(W_fc.T)
    d["wprT"] = np.ascontiguousarray(W_proj.T)
    ep = np.empty((128, 2), np.float32)
    ep[:, 0] = LN_EPS
    ep[:, 1] = RMS_EPS
    d["epsc"] = ep
    d["ident"] = np.eye(128, dtype=BF)
    d["onesrow"] = np.ones((1, 128), BF)
    d["onescol"] = np.ones((128, 1), BF)
    gm = np.zeros((128, 128), np.float32)
    gm[:64, :64] = 1.0 / 64
    gm[64:, 64:] = 1.0 / 64
    d["gmask"] = gm.astype(BF)
    ind = np.zeros((2, 128), np.float32)
    ind[0, :64] = 1.0
    ind[1, 64:] = 1.0
    d["indic2"] = ind.astype(BF)
    cm = np.zeros((128, 4 * 512), np.float32)
    kk = np.arange(128)[:, None]
    qq = np.arange(512)[None, :]
    for o in range(4):
        cm[:, o * 512:(o + 1) * 512] = np.where(kk + o * 128 <= qq, 0.0, NEG)
    d["cmask"] = cm.astype(BF)
    half = 64
    inv_freq = (10000.0 ** (-(np.arange(half, dtype=np.float32)) / np.float32(half))).astype(np.float32)
    pos = np.arange(L, dtype=np.float32)
    ang = (inv_freq[:, None] * pos[None, :]).astype(np.float32)
    cosv = np.cos(ang).astype(np.float32)
    sinv = np.sin(ang).astype(np.float32)
    d["ropec"] = np.concatenate([cosv, cosv], axis=0)
    d["ropes"] = np.concatenate([sinv, sinv], axis=0)
    return d


_NC_CACHE = {}


def kernel(L0_attn_x, L0_attn_wq, L0_attn_wk, L0_attn_wv, L0_attn_wo,
           L0_attn_bq, L0_attn_bk, L0_attn_bv, L0_ffn_W_fc, L0_ffn_W_proj):
    x = np.asarray(L0_attn_x, np.float32)
    args = [np.asarray(a, np.float32) for a in
            (L0_attn_wq, L0_attn_wk, L0_attn_wv, L0_attn_wo,
             L0_attn_bq, L0_attn_bk, L0_attn_bv, L0_ffn_W_fc, L0_ffn_W_proj)]
    if "nc" not in _NC_CACHE:
        _NC_CACHE["nc"] = build_program()
    nc, _ = _NC_CACHE["nc"]
    in_maps = [_host_inputs(x, *args, core=c) for c in range(NCORES)]
    res = run_bass_kernel_spmd(nc, in_maps, core_ids=list(range(NCORES)))
    out = np.empty((16, L, E), np.float32)
    for c in range(NCORES):
        out[c * B_PER:(c + 1) * B_PER] = np.asarray(res.results[c]["out"]).reshape(B_PER, L, E)
    return out


if __name__ == "__main__":
    import time
    t0 = time.time()
    nc, p = build_program()
    print("build ok: %.1fs, %d instructions" % (time.time() - t0, p.ninstr))
    print("sem counts:", p.counts)
